# version 01
# baseline (speedup 1.0000x reference)
import contextlib
import math
import numpy as np
import concourse.bass as bass
import concourse.mybir as mybir
from concourse.bass_utils import run_bass_kernel_spmd

F32 = mybir.dt.float32
AX = mybir.AxisListType.X
OP = mybir.AluOpType
ACT = mybir.ActivationFunctionType

T = 2048
D = 4096
KVD = 1024
NH = 8
HD = 128
S = 8192
E = 8
R = 32
INV_SQRT_HD = 1.0 / math.sqrt(HD)

ENGS = ("sync", "tensor", "vector", "scalar", "gpsimd")


class Sem:
    def __init__(self, handle):
        self.h = handle
        self.n = 0


class Prog:
    def __init__(self):
        self.q = {e: [] for e in ENGS}

    def em(self, eng, f):
        self.q[eng].append(f)

    def wait(self, eng, sem, val=None):
        v = sem.n if val is None else val
        if v <= 0:
            return
        self.em(eng, lambda e, s=sem.h, v=v: e.wait_ge(s, v))

    def dma(self, eng, out, in_, sem):
        sem.n += 16
        self.em(eng, lambda e, o=out, i=in_, s=sem.h: e.dma_start(out=o, in_=i).then_inc(s, 16))

    def op(self, eng, fn, sem=None, inc=1):
        if sem is None:
            self.em(eng, fn)
        else:
            sem.n += inc
            self.em(eng, lambda e, f=fn, s=sem.h, k=inc: f(e).then_inc(s, k))

    def vop(self, fn, sem=None):
        self.op("vector", fn, sem)
        self.em("vector", lambda e: e.drain())


def build(dbg: bool = False, stage: int = 4):
    nc = bass.Bass("TRN2", target_bir_lowering=False)

    x_sl = nc.dram_tensor("x_sl", [T, D], F32, kind="ExternalInput").ap()
    wq = nc.dram_tensor("wq", [D, D], F32, kind="ExternalInput").ap()
    wk = nc.dram_tensor("wk", [D, KVD], F32, kind="ExternalInput").ap()
    wv = nc.dram_tensor("wv", [D, KVD], F32, kind="ExternalInput").ap()
    wo = nc.dram_tensor("wo", [KVD, D], F32, kind="ExternalInput").ap()
    wT = nc.dram_tensor("wT", [D, D], F32, kind="ExternalInput").ap()
    w8 = nc.dram_tensor("w8", [E, D], F32, kind="ExternalInput").ap()
    bvec = nc.dram_tensor("bvec", [1, D], F32, kind="ExternalInput").ap()
    amat = nc.dram_tensor("amat", [D, R], F32, kind="ExternalInput").ap()
    bmat = nc.dram_tensor("bmat", [R, D], F32, kind="ExternalInput").ap()
    limes = nc.dram_tensor("limes", [E, R], F32, kind="ExternalInput").ap()
    limesT = nc.dram_tensor("limesT", [R, E], F32, kind="ExternalInput").ap()
    identH = nc.dram_tensor("identH", [128, 128], F32, kind="ExternalInput").ap()
    out_sl = nc.dram_tensor("out_sl", [T, D], F32, kind="ExternalOutput").ap()

    xnT_h = nc.dram_tensor("xnT_h", [D, T], F32).ap()
    qT_h = nc.dram_tensor("qT_h", [KVD, T], F32).ap()
    attnT_h = nc.dram_tensor("attnT_h", [KVD, T], F32).ap()
    cc_in = nc.dram_tensor("cc_in", [2 * KVD, T], F32).ap()
    cc_out = nc.dram_tensor("cc_out", [4 * 2 * KVD, T], F32).ap()
    ar_in = nc.dram_tensor("ar_in", [1, 2], F32).ap()
    ar_out = nc.dram_tensor("ar_out", [1, 2], F32).ap()

    dbg_outs = {}
    if dbg:
        for nm, shp in [("d_row", [1, 64]), ("d_sm", [128, 4]),
                        ("d_xnT", [D, T]), ("d_zT", [R, T]), ("d_Hs", [128, 64]),
                        ("d_Ds", [128, 64]), ("d_scales", [1, 2]), ("d_qT", [KVD, T]),
                        ("d_kv", [4 * 2 * KVD, T]), ("d_attnT", [KVD, T]),
                        ("d_zpm", [R, T]), ("d_wfT", [E, T])]:
            dbg_outs[nm] = nc.dram_tensor(nm, shp, F32, kind="ExternalOutput").ap()

    def sb(name, shape):
        return nc.alloc_sbuf_tensor(name, shape, F32)

    big0 = sb("big0", [128, 16384])
    big1 = sb("big1", [128, 16448])
    qtb = sb("qtb", [128, 4096])
    aring = sb("aring", [128, 2048])
    zT = sb("zT", [R, T])
    zpm = sb("zpm", [R, T])
    ident = sb("ident", [128, 128])
    ones1 = sb("ones1", [1, 128])
    negones1 = sb("negones1", [1, 128])
    ones_col = sb("ones_col", [128, 1])
    b_sb = sb("b_sb", [1, D])
    bm_t = [sb("bm_t0", [R, 512]), sb("bm_t1", [R, 512])]
    limes_sb = sb("limes_sb", [E, R])
    limesh_sb = sb("limesh_sb", [E, R])
    limesT_sb = sb("limesT_sb", [R, E])
    amat_sb = sb("amat_sb", [128, 32 * R])
    w8t_sb = sb("w8t_sb", [128, 32 * E])
    Hs = sb("Hs", [128, 64])
    Ds = sb("Ds", [128, 64])
    stats = sb("stats", [128, 32])
    small = sb("small", [128, 32])
    row = sb("row", [1, 64])
    sq0 = sb("sq0", [128, 256])
    sq1 = sb("sq1", [128, 256])
    negC = sb("negC", [128, 1])
    astage = sb("astage", [128, 128])
    astage2 = sb("astage2", [128, 128])
    gat = sb("gat", [128, 448])
    scal2 = sb("scal2", [1, 2])

    wfT = aring.ap()[0:E, 0:T]

    pk = [nc.alloc_psum_tensor(f"pk{i}", [128, 512], F32) for i in range(8)]

    es = contextlib.ExitStack()
    LD = Sem(es.enter_context(nc.semaphore("LD")))
    ST = Sem(es.enter_context(nc.semaphore("ST")))
    TT = Sem(es.enter_context(nc.semaphore("TT")))
    VV = Sem(es.enter_context(nc.semaphore("VV")))
    SS = Sem(es.enter_context(nc.semaphore("SS")))
    CC = Sem(es.enter_context(nc.semaphore("CC")))
    SC = Sem(es.enter_context(nc.semaphore("SC")))
    EX = Sem(es.enter_context(nc.semaphore("EX")))
    AVK = Sem(es.enter_context(nc.semaphore("AVK")))

    p = Prog()

    def fin():
        p.wait("gpsimd", ST)
        with nc.Block() as block:
            @block.sync
            def _(e):
                for f in p.q["sync"]:
                    f(e)

            @block.tensor
            def _(e):
                for f in p.q["tensor"]:
                    f(e)

            @block.vector
            def _(e):
                for f in p.q["vector"]:
                    f(e)

            @block.scalar
            def _(e):
                for f in p.q["scalar"]:
                    f(e)

            @block.gpsimd
            def _(e):
                for f in p.q["gpsimd"]:
                    f(e)
        es.close()
        return nc, dbg_outs

    p.op("vector", lambda e: e.memset(ones1[:, :], 1.0))
    p.op("vector", lambda e: e.memset(negones1[:, :], -1.0))
    p.op("vector", lambda e: e.memset(ones_col[:, :], 1.0))
    w8_sb = big0[0:8, 0:4096]
    for dst, src in [(ident[:, :], identH), (b_sb[:, :], bvec),
                     (limes_sb[:, :], limes), (limesT_sb[:, :], limesT), (w8_sb, w8)]:
        p.dma("sync", dst, src, LD)
    p.dma("sync", amat_sb.ap().rearrange("p (c r) -> p c r", c=32, r=R),
          amat.rearrange("(c p) r -> p c r", p=128), LD)
    setup_ld = LD.n
    p.wait("vector", LD, setup_ld)
    p.op("vector", lambda e: e.tensor_scalar_mul(limesh_sb[:, :], limes_sb[:, :], 0.5), VV)
    p.wait("tensor", LD, setup_ld)
    w8t_vv = []
    for d in range(32):
        if d >= 2:
            p.wait("tensor", VV, w8t_vv[d - 2])
        p.op("tensor", lambda e, d=d: e.transpose(
            pk[6 + d % 2][0:128, 0:8], w8_sb[0:8, 128 * d:128 * d + 128], ident[0:8, 0:8]), TT)
        p.wait("vector", TT)
        p.op("vector", lambda e, d=d: e.tensor_copy(w8t_sb[:, 8 * d:8 * d + 8], pk[6 + d % 2][0:128, 0:8]), VV)
        w8t_vv.append(VV.n)
    p.wait("tensor", VV)
    w8_done_tt = TT.n

    x3 = x_sl.rearrange("(i h p) d -> i p h d", h=2, p=128)
    xn_ss = []
    tr_tt = []
    st_after_tile = []
    for i in range(8):
        xb = big0[:, 8192 * (i % 2):8192 * (i % 2) + 8192]
        if i == 0:
            p.wait("sync", TT, w8_done_tt)
        if i >= 2:
            p.wait("sync", SS, xn_ss[i - 2])
        p.dma("sync", xb.rearrange("p (h d) -> p h d", h=2, d=D), x3[i], LD)
        ld_tile = LD.n
        st = stats
        p.wait("vector", LD, ld_tile)
        p.op("vector", lambda e, xb=xb: e.reduce_sum(
            st[:, 0:2], xb.rearrange("p (h d) -> p h d", h=2, d=D), AX), VV)
        p.wait("scalar", LD, ld_tile)
        if i >= 1:
            p.wait("scalar", TT, tr_tt[i - 1])
            p.wait("scalar", ST, st_after_tile[i - 1])
        for h in range(2):
            p.op("scalar", lambda e, xb=xb, h=h: e.activation(
                big1[:, 8192 + 4096 * h:8192 + 4096 * h + 4096],
                xb[:, 4096 * h:4096 * h + 4096], ACT.Square, accum_out=st[:, 2 + h:3 + h]), SS)
        p.wait("vector", SS)
        p.em("vector", lambda e: e.drain())
        p.vop(lambda e: e.tensor_scalar_mul(st[:, 0:2], st[:, 0:2], 1.0 / D))
        p.vop(lambda e: e.tensor_scalar_mul(st[:, 2:4], st[:, 2:4], 1.0 / D))
        p.vop(lambda e: e.tensor_tensor(st[:, 4:6], st[:, 0:2], st[:, 0:2], OP.mult))
        p.vop(lambda e: e.tensor_sub(st[:, 4:6], st[:, 2:4], st[:, 4:6]))
        p.vop(lambda e: e.tensor_scalar_add(st[:, 10:12], st[:, 4:6], 1e-5), VV)
        p.wait("scalar", VV)
        p.op("scalar", lambda e: e.activation(st[:, 12:14], st[:, 10:12], ACT.Sqrt), SS)
        p.wait("vector", SS)
        p.vop(lambda e: e.reciprocal(st[:, 6:8], st[:, 12:14]))
        p.vop(lambda e: e.tensor_tensor(st[:, 14:16], st[:, 6:8], st[:, 6:8], OP.mult))
        p.vop(lambda e: e.tensor_tensor(st[:, 14:16], st[:, 14:16], st[:, 10:12], OP.mult))
        p.vop(lambda e: e.tensor_scalar(st[:, 14:16], st[:, 14:16], -0.5, 1.5, OP.mult, OP.add))
        p.vop(lambda e: e.tensor_tensor(st[:, 6:8], st[:, 6:8], st[:, 14:16], OP.mult))
        p.vop(lambda e: e.tensor_tensor(st[:, 8:10], st[:, 0:2], st[:, 6:8], OP.mult))
        p.vop(lambda e: e.tensor_scalar_mul(st[:, 8:10], st[:, 8:10], -1.0), VV)
        p.wait("scalar", VV)
        for h in range(2):
            p.op("scalar", lambda e, xb=xb, h=h: e.activation(
                big1[:, 4096 * h:4096 * h + 4096], xb[:, 4096 * h:4096 * h + 4096], ACT.Identity,
                bias=st[:, 8 + h:9 + h], scale=st[:, 6 + h:7 + h]), SS)
        xn_ss.append(SS.n)
        p.wait("tensor", SS, xn_ss[-1])
        if i >= 1:
            p.wait("tensor", ST, st_after_tile[i - 1])
        evac_vv = []
        for c in range(64):
            bank = 4 + (c % 4)
            if c >= 4:
                p.wait("tensor", VV, evac_vv[c - 4])
            p.op("tensor", lambda e, c=c, bank=bank: e.transpose(
                pk[bank][:, 0:128], big1[:, 128 * c:128 * c + 128], ident[:, :]), TT)
            p.wait("vector", TT)
            h_, d_ = c // 32, c % 32
            p.op("vector", lambda e, h_=h_, d_=d_, bank=bank: e.tensor_copy(
                big1[:, 8192 + 256 * d_ + 128 * h_:8192 + 256 * d_ + 128 * h_ + 128],
                pk[bank][:, 0:128]), VV)
            evac_vv.append(VV.n)
        tr_tt.append(TT.n)
        xnT_tile = big1[:, 8192:16384]
        p.wait("tensor", VV, evac_vv[-1])
        for d in range(32):
            p.op("tensor", lambda e, d=d: e.matmul(
                pk[0][0:R, 0:256], amat_sb[:, 32 * d:32 * d + 32],
                xnT_tile[:, 256 * d:256 * d + 256], start=(d == 0), stop=(d == 31)),
                TT if d == 31 else None)
        for d in range(32):
            p.op("tensor", lambda e, d=d: e.matmul(
                pk[1][0:128, 0:8], xnT_tile[:, 256 * d + 1:256 * d + 256:2],
                w8t_sb[:, 8 * d:8 * d + 8], start=(d == 0), stop=(d == 31)),
                TT if d == 31 else None)
        p.wait("vector", TT)
        p.op("vector", lambda e, i=i: e.tensor_copy(zT[:, 256 * i:256 * i + 256], pk[0][0:R, 0:256]), VV)
        p.op("vector", lambda e, i=i: e.tensor_copy(Hs[:, 8 * i:8 * i + 8], pk[1][0:128, 0:8]), VV)
        p.wait("gpsimd", VV)
        for gq in range(4):
            src = xnT_tile[:, 2048 * gq:2048 * gq + 2048].rearrange("p (c j) -> p c j", c=8, j=256)
            dst = xnT_h.rearrange("(c p) t -> p c t", p=128)[:, 8 * gq:8 * gq + 8, 256 * i:256 * i + 256]
            p.dma("gpsimd", dst, src, ST)
        st_after_tile.append(ST.n)
        p.wait("tensor", VV)

    for c in range(8):
        p.op("tensor", lambda e, c=c: e.matmul(
            pk[2][0:128, 0:8], zT[:, 256 * c + 1:256 * c + 256:2], limesT_sb[:, :],
            start=True, stop=True), TT)
        p.wait("vector", TT)
        p.op("vector", lambda e, c=c: e.tensor_copy(Ds[:, 8 * c:8 * c + 8], pk[2][0:128, 0:8]), VV)
        p.wait("tensor", VV)
    for j, buf in ((0, Hs), (1, Ds)):
        p.vop(lambda e, buf=buf: e.reduce_max(
            small[:, 0:1], buf[:, :], AX, apply_absolute_value=True), VV)
        p.wait("tensor", VV)
        p.op("tensor", lambda e: e.transpose(pk[3][0:1, 0:128], small[:, 0:1], ident[:, :]), TT)
        p.wait("vector", TT)
        p.vop(lambda e, j=j: e.reduce_max(row[0:1, j:j + 1], pk[3][0:1, 0:128], AX))
        p.vop(lambda e, j=j: e.tensor_scalar_max(row[0:1, j:j + 1], row[0:1, j:j + 1], 1e-6), VV)
        p.wait("tensor", VV)
    p.wait("gpsimd", VV)
    p.dma("gpsimd", ar_in, row[0:1, 0:2], ST)

    if dbg:
        p.wait("gpsimd", VV)
        p.dma("gpsimd", dbg_outs["d_zT"], zT[:, :], ST)
        p.dma("gpsimd", dbg_outs["d_Hs"], Hs[:, :], ST)
        p.dma("gpsimd", dbg_outs["d_Ds"], Ds[:, :], ST)
        p.dma("gpsimd", dbg_outs["d_xnT"], xnT_h, ST)
        p.wait("gpsimd", ST)
        p.wait("gpsimd", TT, TT.n)
        p.wait("gpsimd", VV, VV.n)
        p.wait("gpsimd", SS, SS.n)
        p.op("gpsimd", lambda e: e.collective_compute(
            "AllReduce", OP.max, replica_groups=[[0, 1, 2, 3, 4, 5, 6, 7]],
            ins=[ar_in.opt()], outs=[ar_out.opt()]), CC)
        p.wait("gpsimd", CC)
        p.dma("gpsimd", dbg_outs["d_scales"], ar_out, ST)
        p.dma("gpsimd", dbg_outs["d_row"], row[0:1, 0:64], ST)
        p.dma("gpsimd", dbg_outs["d_sm"], small[:, 0:4], ST)
    if stage <= 1:
        return fin()

    xnT_src = xnT_h.rearrange("(c p) t -> p c t", p=128)
    slice_tt = []
    evac_hist = []
    stg_st = []

    p.wait("sync", SS, xn_ss[-1])
    p.wait("sync", ST, st_after_tile[-1])
    p.wait("sync", TT, tr_tt[-1])

    def emit_proj(weight_ap, dst_ap, fold=False):
        for half in range(2):
            if not fold:
                src = weight_ap[:, 512 * half:512 * half + 512].rearrange("(c p) o -> p c o", p=128)
                dstw = big1[:, 0:16384].rearrange("p (c o) -> p c o", c=32, o=512)
                if slice_tt:
                    p.wait("sync", TT, slice_tt[-1])
                for gq in range(4):
                    p.dma("sync", dstw[:, 8 * gq:8 * gq + 8, :], src[:, 8 * gq:8 * gq + 8, :], LD)
                p.wait("tensor", LD, LD.n)
            else:
                fold_vv = []
                if slice_tt:
                    p.wait("sync", TT, slice_tt[-1])
                    p.wait("vector", TT, slice_tt[-1])
                for d in range(32):
                    tb = qtb[:, 2048 * (d % 2):2048 * (d % 2) + 2048]
                    if d >= 2:
                        p.wait("sync", VV, fold_vv[d - 2])
                    p.dma("sync", tb, wq[128 * d:128 * d + 128, 2048 * half:2048 * half + 2048], LD)
                    p.wait("vector", LD)
                    v4 = tb.rearrange("p (kv g j) -> p kv g j", kv=4, g=4, j=128)
                    o4 = big1[:, 512 * d:512 * d + 512].rearrange("p (kv j) -> p kv j", kv=4, j=128)
                    p.op("vector", lambda e, v4=v4, o4=o4: e.tensor_tensor(o4, v4[:, :, 0, :], v4[:, :, 1, :], OP.add))
                    p.op("vector", lambda e, v4=v4, o4=o4: e.tensor_tensor(o4, o4, v4[:, :, 2, :], OP.add))
                    p.op("vector", lambda e, v4=v4, o4=o4: e.tensor_tensor(o4, o4, v4[:, :, 3, :], OP.add), VV)
                    fold_vv.append(VV.n)
                p.wait("tensor", VV, fold_vv[-1])
            for tsl in range(8):
                xb = big0[:, 8192 * (tsl % 2):8192 * (tsl % 2) + 8192]
                if len(slice_tt) >= 2:
                    p.wait("sync", TT, slice_tt[-2])
                xdst = xb.rearrange("p (c j) -> p c j", c=32, j=256)
                for gq in range(4):
                    p.dma("sync", xdst[:, 8 * gq:8 * gq + 8, :],
                          xnT_src[:, 8 * gq:8 * gq + 8, 256 * tsl:256 * tsl + 256], LD)
                sld = LD.n
                p.wait("tensor", LD, sld)
                for oc in range(4):
                    bank = oc % 2
                    if len(evac_hist) >= 2:
                        p.wait("tensor", VV, evac_hist[-2])
                    for d in range(32):
                        p.op("tensor", lambda e, d=d, oc=oc, bank=bank, xb=xb: e.matmul(
                            pk[bank][:, 0:256], big1[:, 512 * d + 128 * oc:512 * d + 128 * oc + 128],
                            xb[:, 256 * d:256 * d + 256], start=(d == 0), stop=(d == 31)),
                            TT if d == 31 else None)
                    p.wait("vector", TT)
                    stg = aring[:, 512 * (oc % 2):512 * (oc % 2) + 256]
                    if len(stg_st) >= 2:
                        p.wait("vector", ST, stg_st[-2])
                    p.op("vector", lambda e, stg=stg, bank=bank: e.tensor_copy(stg, pk[bank][:, 0:256]), VV)
                    evac_hist.append(VV.n)
                    p.wait("gpsimd", VV)
                    orow = 512 * half + 128 * oc
                    p.dma("gpsimd", dst_ap[orow:orow + 128, 256 * tsl:256 * tsl + 256], stg, ST)
                    stg_st.append(ST.n)
                slice_tt.append(TT.n)

    emit_proj(wk, cc_in[0:KVD, :])
    emit_proj(wv, cc_in[KVD:2 * KVD, :])
    emit_proj(None, qT_h, fold=True)

    p.wait("gpsimd", ST)
    p.wait("gpsimd", TT, TT.n)
    p.wait("gpsimd", VV, VV.n)
    p.wait("gpsimd", SS, SS.n)
    p.wait("gpsimd", LD, LD.n)
    p.op("gpsimd", lambda e: e.collective_compute(
        "AllReduce", OP.max, replica_groups=[[0, 1, 2, 3, 4, 5, 6, 7]],
        ins=[ar_in.opt()], outs=[ar_out.opt()]), CC)
    p.wait("gpsimd", CC)
    p.op("gpsimd", lambda e: e.collective_compute(
        "AllGather", OP.bypass, replica_groups=[[0, 1, 2, 3], [4, 5, 6, 7]],
        ins=[cc_in.opt()], outs=[cc_out.opt()]), CC)

    if dbg:
        p.wait("gpsimd", ST)
        p.wait("gpsimd", CC)
        p.dma("gpsimd", dbg_outs["d_qT"], qT_h, ST)
        p.dma("gpsimd", dbg_outs["d_kv"], cc_out, ST)
    if stage <= 2:
        return fin()

    p.wait("sync", CC, CC.n)
    p.wait("sync", TT, slice_tt[-1])
    vones = big1[:, 8192:16448]
    vtraw = big1[:, 0:8192]
    p.wait("vector", TT, slice_tt[-1])
    p.op("vector", lambda e: e.memset(
        vones.rearrange("p (c j) -> p c j", c=64, j=129)[:, :, 128:129], 1.0), VV)
    vones_init_vv = VV.n

    kt_consumed = [0] * NH
    vt_consumed = [0] * NH
    av_done = [0] * NH
    norm_consumed = [0] * NH

    for h in range(NH):
        ktb = big0[:, 8192 * (h % 2):8192 * (h % 2) + 8192]
        qb = qtb[:, 2048 * (h % 2):2048 * (h % 2) + 2048]
        if h >= 2:
            p.wait("sync", TT, kt_consumed[h - 2])
            p.wait("sync", SS, norm_consumed[h - 2])
        for r_ in range(4):
            p.dma("sync", ktb[:, 2048 * r_:2048 * r_ + 2048],
                  cc_out[2048 * r_ + HD * h:2048 * r_ + HD * h + HD, :], LD)
        if h >= 1:
            p.wait("sync", TT, vt_consumed[h - 1])
        for r_ in range(4):
            p.dma("sync", vtraw[:, 2048 * r_:2048 * r_ + 2048],
                  cc_out[2048 * r_ + KVD + HD * h:2048 * r_ + KVD + HD * h + HD, :], LD)
        p.dma("sync", qb, qT_h[HD * h:HD * h + HD, :], LD)
        ld_head = LD.n

        p.wait("tensor", LD, ld_head)
        if h >= 1:
            p.wait("tensor", AVK, av_done[h - 1])
        p.wait("tensor", VV)
        evac_vv = []
        for c in range(64):
            bank = 6 + (c % 2)
            if c >= 2:
                p.wait("tensor", VV, evac_vv[c - 2])
            p.op("tensor", lambda e, c=c, bank=bank: e.transpose(
                pk[bank][:, 0:128], vtraw[:, 128 * c:128 * c + 128], ident[:, :]), TT)
            p.wait("vector", TT)
            p.op("vector", lambda e, c=c, bank=bank: e.tensor_copy(
                vones[:, 129 * c:129 * c + 128], pk[bank][:, 0:128]), VV)
            evac_vv.append(VV.n)
        vt_consumed[h] = TT.n
        p.wait("tensor", VV, evac_vv[-1])

        p.vop(lambda e: e.memset(row[0:1, 2:4], 0.0), VV)
        p.wait("scalar", LD, ld_head)
        p.wait("scalar", VV)
        mm_tt = []
        for which, buf, ntile, slot in (("k", ktb, 32, 2), ("q", qb, 8, 3)):
            for t in range(ntile):
                sqb = sq0 if t % 2 == 0 else sq1
                if len(mm_tt) >= 2:
                    p.wait("scalar", TT, mm_tt[-2])
                p.op("scalar", lambda e, buf=buf, t=t, sqb=sqb: e.activation(
                    sqb[:, :], buf[:, 256 * t:256 * t + 256], ACT.Square), SS)
                p.wait("tensor", SS)
                p.wait("tensor", VV)
                p.op("tensor", lambda e, sqb=sqb: e.matmul(
                    pk[5][0:1, 0:256], ones_col[:, :], sqb[:, :], start=True, stop=True), TT)
                mm_tt.append(TT.n)
                p.wait("vector", TT)
                p.vop(lambda e: e.reduce_max(row[0:1, 4:5], pk[5][0:1, 0:256], AX))
                p.vop(lambda e, slot=slot: e.tensor_tensor(
                    row[0:1, slot:slot + 1], row[0:1, slot:slot + 1], row[0:1, 4:5], OP.max), VV)
        norm_consumed[h] = SS.n
        p.vop(lambda e: e.tensor_tensor(row[0:1, 5:6], row[0:1, 2:3], row[0:1, 3:4], OP.mult), VV)
        p.wait("scalar", VV)
        p.op("scalar", lambda e: e.activation(row[0:1, 6:7], row[0:1, 5:6], ACT.Sqrt), SS)
        p.wait("vector", SS)
        p.vop(lambda e: e.tensor_scalar_mul(row[0:1, 6:7], row[0:1, 6:7], 1.01 * INV_SQRT_HD), VV)
        p.wait("tensor", VV)
        p.op("tensor", lambda e: e.matmul(pk[5][0:128, 0:1], negones1[:, :], row[0:1, 6:7],
                                          start=True, stop=True), TT)
        p.wait("vector", TT)
        p.wait("vector", EX, EX.n)
        p.op("vector", lambda e: e.tensor_copy(negC[:, :], pk[5][0:128, 0:1]), VV)
        negc_vv = VV.n
        p.wait("scalar", VV, negc_vv)
        p.wait("tensor", VV, negc_vv)

        for qt in range(4):
            sc_base = SC.n
            ex_base = EX.n
            avk_base = AVK.n
            epi_vv = VV.n
            for kc in range(64):
                sbank = 4 + (kc % 3)
                if kc >= 3:
                    p.wait("tensor", EX, ex_base + kc - 2)
                p.op("tensor", lambda e, kc=kc, qt=qt, sbank=sbank, ktb=ktb, qb=qb: e.matmul(
                    pk[sbank][:, 0:512], ktb[:, 128 * kc:128 * kc + 128],
                    qb[:, 512 * qt:512 * qt + 512], start=True, stop=True), SC)
                p.wait("scalar", SC, sc_base + kc + 1)
                if kc >= 4:
                    p.wait("scalar", AVK, avk_base + kc - 3)
                p.op("scalar", lambda e, kc=kc, sbank=sbank: e.activation(
                    aring[:, 512 * (kc % 4):512 * (kc % 4) + 512], pk[sbank][:, 0:512],
                    ACT.Exp, bias=negC[:, 0:1], scale=INV_SQRT_HD), EX)
                p.wait("tensor", EX, ex_base + kc + 1)
                for qs in range(4):
                    p.op("tensor", lambda e, kc=kc, qs=qs: e.matmul(
                        pk[qs][:, 0:129],
                        aring[:, 512 * (kc % 4) + 128 * qs:512 * (kc % 4) + 128 * qs + 128],
                        vones[:, 129 * kc:129 * kc + 129], start=(kc == 0), stop=(kc == 63)),
                        AVK if qs == 3 else None)
            p.wait("vector", AVK, AVK.n)
            for qs in range(4):
                p.vop(lambda e, qs=qs: e.tensor_copy(small[:, 8:9], pk[qs][:, 128:129]))
                p.vop(lambda e: e.reciprocal(small[:, 9:10], small[:, 8:9]))
                p.op("vector", lambda e, qs=qs: e.tensor_scalar(
                    astage[:, :], pk[qs][:, 0:128], small[0:128, 9:10], None, OP.mult), VV)
                p.wait("tensor", VV)
                p.op("tensor", lambda e: e.transpose(pk[7][:, 0:128], astage[:, :], ident[:, :]), TT)
                p.wait("vector", TT)
                p.wait("vector", ST)
                p.op("vector", lambda e: e.tensor_copy(astage2[:, :], pk[7][:, 0:128]), VV)
                p.wait("gpsimd", VV)
                p.dma("gpsimd", attnT_h[HD * h:HD * h + HD,
                                        512 * qt + 128 * qs:512 * qt + 128 * qs + 128], astage2[:, :], ST)
            p.wait("tensor", VV)
        kt_consumed[h] = TT.n
        av_done[h] = AVK.n

    if dbg:
        p.wait("gpsimd", ST)
        p.dma("gpsimd", dbg_outs["d_attnT"], attnT_h, ST)
    if stage <= 3:
        return fin()

    p.wait("gpsimd", CC, CC.n)
    p.dma("gpsimd", scal2[0:1, 0:2], ar_out, LD)
    gld = LD.n
    p.wait("vector", LD, gld)
    p.vop(lambda e: e.reciprocal(row[0:1, 8:10], scal2[0:1, 0:2]))
    p.vop(lambda e: e.tensor_scalar_mul(row[0:1, 8:10], row[0:1, 8:10], 0.5), VV)
    for j in range(2):
        p.wait("tensor", VV)
        p.op("tensor", lambda e, j=j: e.matmul(pk[7][0:128, 0:1], ones1[:, :], row[0:1, 8 + j:9 + j],
                                               start=True, stop=True), TT)
        p.wait("vector", TT)
        p.vop(lambda e, j=j: e.tensor_copy(small[:, 10 + j:11 + j], pk[7][0:128, 0:1]), VV)
        p.wait("tensor", VV)
    L = gat[:, 0:64]
    P_ = gat[:, 64:128]
    p.vop(lambda e: e.tensor_scalar(L, Hs[:, :], small[0:128, 10:11], None, OP.mult))
    p.vop(lambda e: e.scalar_tensor_tensor(L, Ds[:, :], small[0:128, 11:12], L, OP.mult, OP.add), VV)
    p.wait("scalar", VV)
    p.op("scalar", lambda e: e.activation(P_, L, ACT.Exp), SS)
    p.wait("vector", SS)
    g3 = P_.rearrange("p (c e) -> p c e", c=8, e=E)
    p.vop(lambda e: e.reduce_sum(small[:, 12:20], g3, AX))
    p.vop(lambda e: e.reciprocal(small[:, 20:28], small[:, 12:20]))
    for c in range(8):
        p.vop(lambda e, c=c: e.tensor_scalar(
            P_[:, 8 * c:8 * c + 8], P_[:, 8 * c:8 * c + 8], small[0:128, 20 + c:21 + c], None, OP.mult))
    for c in range(8):
        p.vop(lambda e, c=c: e.max(out=gat[:, 192 + 8 * c:192 + 8 * c + 8],
                                   in_=P_[:, 8 * c:8 * c + 8]))
    for c in range(8):
        p.vop(lambda e, c=c: e.tensor_scalar(
            gat[:, 256 + 8 * c:256 + 8 * c + 8], P_[:, 8 * c:8 * c + 8],
            gat[0:128, 192 + 8 * c + 1:192 + 8 * c + 2], None, OP.subtract), VV)
    p.wait("scalar", VV)
    p.op("scalar", lambda e: e.activation(gat[:, 320:384], gat[:, 256:320], ACT.Sigmoid, scale=2.0), SS)
    p.wait("vector", SS)
    W_ = gat[:, 384:448]
    p.vop(lambda e: e.tensor_tensor(W_, P_, gat[:, 320:384], OP.mult))
    w3 = W_.rearrange("p (c e) -> p c e", c=8, e=E)
    p.vop(lambda e: e.reduce_sum(small[:, 12:20], w3, AX))
    p.vop(lambda e: e.tensor_scalar_add(small[:, 12:20], small[:, 12:20], 1e-9))
    p.vop(lambda e: e.reciprocal(small[:, 20:28], small[:, 12:20]))
    for c in range(8):
        p.vop(lambda e, c=c: e.tensor_scalar(
            W_[:, 8 * c:8 * c + 8], W_[:, 8 * c:8 * c + 8], small[0:128, 20 + c:21 + c], None, OP.mult), VV)
    for c in range(8):
        p.wait("tensor", VV)
        p.op("tensor", lambda e, c=c: e.transpose(pk[7][0:8, 0:128], W_[:, 8 * c:8 * c + 8], ident[:, :]), TT)
        p.wait("vector", TT)
        p.op("vector", lambda e, c=c: e.tensor_copy(wfT[0:8, 256 * c:256 * c + 256:2], pk[7][0:8, 0:128]))
        p.op("vector", lambda e, c=c: e.tensor_copy(wfT[0:8, 256 * c + 1:256 * c + 256:2], pk[7][0:8, 0:128]), VV)
    p.wait("tensor", VV)
    for c in range(4):
        p.op("tensor", lambda e, c=c: e.matmul(
            pk[3][0:R, 0:512], limesh_sb[:, :], wfT[0:8, 512 * c:512 * c + 512],
            start=True, stop=True), TT)
        p.wait("vector", TT)
        p.op("vector", lambda e, c=c: e.tensor_tensor(
            zpm[:, 512 * c:512 * c + 512], pk[3][0:R, 0:512], zT[:, 512 * c:512 * c + 512], OP.mult), VV)
        p.wait("tensor", VV)

    if dbg:
        p.wait("gpsimd", VV)
        p.dma("gpsimd", dbg_outs["d_zpm"], zpm[:, :], ST)
        p.dma("gpsimd", dbg_outs["d_wfT"], wfT[0:E, 0:T], ST)

    wo_src = wo.rearrange("(c p) o -> p c o", p=128)
    wT_src = wT.rearrange("(c p) o -> p c o", p=128)
    attnT_src = attnT_h.rearrange("(c p) t -> p c t", p=128)

    p.wait("sync", TT, TT.n)
    p.wait("sync", AVK, AVK.n)
    p.wait("sync", VV, VV.n)
    p.wait("tensor", VV, VV.n)

    blk_tt = []
    hd_tt = []
    smm_tt = []
    wt_hist = []
    silu_ss = []
    sg_vv = []
    xst_vv = []

    for tb in range(4):
        if tb >= 1:
            p.wait("sync", TT, blk_tt[-1])
        for gq in range(4):
            p.dma("sync", big0.ap().rearrange("p (c j) -> p c j", c=32, j=512)[:, 8 * gq:8 * gq + 8, :],
                  xnT_src[:, 8 * gq:8 * gq + 8, 512 * tb:512 * tb + 512], LD)
        p.dma("sync", qtb.ap().rearrange("p (c j) -> p c j", c=8, j=512),
              attnT_src[:, :, 512 * tb:512 * tb + 512], LD)
        blk_ld = LD.n
        p.wait("tensor", LD, blk_ld)
        for o in range(8):
            wop = big1[:, 4096 * (o % 2):4096 * (o % 2) + 4096]
            if len(smm_tt) >= 2:
                p.wait("sync", TT, smm_tt[-2])
            p.dma("sync", wop.rearrange("p (c j) -> p c j", c=8, j=512),
                  wo_src[:, :, 512 * o:512 * o + 512], LD)
            wo_ld = LD.n
            if len(hd_tt) >= 2:
                p.wait("sync", TT, hd_tt[-2])
            p.dma("sync", bm_t[o % 2][:, :], bmat[0:R, 512 * o:512 * o + 512], LD)
            bm_ld = LD.n
            for d in range(32):
                wtp = big1[:, 8192 + 512 * (d % 2):8192 + 512 * (d % 2) + 512]
                if len(wt_hist) >= 2:
                    p.wait("sync", TT, wt_hist[-2])
                p.dma("sync", wtp, wT_src[:, d, 512 * o:512 * o + 512], LD)
                p.wait("tensor", LD, LD.n)
                if d == 0 and silu_ss:
                    p.wait("tensor", SS, silu_ss[-1])
                for ts in range(4):
                    p.op("tensor", lambda e, d=d, ts=ts, wtp=wtp: e.matmul(
                        pk[ts][:, 0:512], big0[:, 512 * d + 128 * ts:512 * d + 128 * ts + 128],
                        wtp, start=(d == 0), stop=False), TT if ts == 3 else None)
                wt_hist.append(TT.n)
            for ts in range(4):
                p.op("tensor", lambda e, ts=ts, o=o: e.matmul(
                    pk[ts][:, 0:512], ones1[:, :], b_sb[0:1, 512 * o:512 * o + 512],
                    start=False, stop=False), None)
                if ts == 0:
                    p.wait("tensor", LD, bm_ld)
                p.op("tensor", lambda e, o=o, ts=ts, tb=tb: e.matmul(
                    pk[ts][:, 0:512], zpm[:, 512 * tb + 128 * ts:512 * tb + 128 * ts + 128],
                    bm_t[o % 2][:, :], start=False, stop=True),
                    TT if ts == 3 else None)
            hdone_tt = TT.n
            hd_tt.append(hdone_tt)
            p.wait("scalar", TT, hdone_tt)
            if sg_vv:
                p.wait("scalar", VV, sg_vv[-1])
            for ts in range(4):
                p.op("scalar", lambda e, ts=ts: e.activation(
                    big1[:, 9216 + 512 * ts:9216 + 512 * ts + 512], pk[ts][:, 0:512], ACT.Silu), SS)
            silu_ss.append(SS.n)
            p.wait("tensor", LD, wo_ld)
            if sg_vv:
                p.wait("tensor", VV, sg_vv[-1])
            for dv in range(8):
                for ts in range(4):
                    p.op("tensor", lambda e, dv=dv, ts=ts, wop=wop: e.matmul(
                        pk[4 + ts][:, 0:512], qtb[:, 512 * dv + 128 * ts:512 * dv + 128 * ts + 128],
                        wop[:, 512 * dv:512 * dv + 512], start=(dv == 0), stop=(dv == 7)),
                        TT if (dv == 7 and ts == 3) else None)
            smm_tt.append(TT.n)
            for ts in range(4):
                xst = aring[:, 512 * (ts % 2):512 * (ts % 2) + 512]
                if len(xst_vv) >= 2:
                    p.wait("sync", VV, xst_vv[-2])
                p.dma("sync", xst, x_sl[512 * tb + 128 * ts:512 * tb + 128 * ts + 128,
                                        512 * o:512 * o + 512], LD)
                xl = LD.n
                if ts == 0:
                    p.wait("vector", TT, smm_tt[-1])
                    p.wait("vector", SS, silu_ss[-1])
                p.wait("vector", LD, xl)
                ost = aring[:, 1024 + 512 * (ts % 2):1024 + 512 * (ts % 2) + 512]
                if ts >= 2:
                    p.wait("vector", ST)
                p.op("vector", lambda e, ts=ts, ost=ost: e.tensor_tensor(
                    ost, pk[4 + ts][:, 0:512], big1[:, 9216 + 512 * ts:9216 + 512 * ts + 512], OP.mult))
                p.op("vector", lambda e, ts=ts, ost=ost, xst=xst: e.tensor_tensor(ost, ost, xst, OP.add), VV)
                xst_vv.append(VV.n)
                p.wait("gpsimd", VV)
                p.dma("gpsimd", out_sl[512 * tb + 128 * ts:512 * tb + 128 * ts + 128,
                                       512 * o:512 * o + 512], ost, ST)
            sg_vv.append(VV.n)
        blk_tt.append(TT.n)

    return fin()


def make_inputs(inputs):
    x = np.ascontiguousarray(np.asarray(inputs["L0_attn_x"], dtype=np.float32))
    W = np.asarray(inputs["L0_ffn_W"], dtype=np.float32)
    common = {
        "wq": np.ascontiguousarray(np.asarray(inputs["L0_attn_wq"], dtype=np.float32)),
        "wk": np.ascontiguousarray(np.asarray(inputs["L0_attn_wk"], dtype=np.float32)),
        "wv": np.ascontiguousarray(np.asarray(inputs["L0_attn_wv"], dtype=np.float32)),
        "wo": np.ascontiguousarray(np.asarray(inputs["L0_attn_wo"], dtype=np.float32)),
        "wT": np.ascontiguousarray(W.T),
        "w8": np.ascontiguousarray(W[0:E, :]),
        "bvec": np.ascontiguousarray(np.asarray(inputs["L0_ffn_b"], dtype=np.float32)).reshape(1, D),
        "amat": np.ascontiguousarray(np.asarray(inputs["L0_ffn_A"], dtype=np.float32)),
        "bmat": np.ascontiguousarray(np.asarray(inputs["L0_ffn_Bmat"], dtype=np.float32)),
        "limes": np.ascontiguousarray(np.asarray(inputs["L0_ffn_LiMEs"], dtype=np.float32)),
        "limesT": np.ascontiguousarray(np.asarray(inputs["L0_ffn_LiMEs"], dtype=np.float32).T),
        "identH": np.eye(128, dtype=np.float32),
    }
    in_maps = []
    for c in range(8):
        b, t0 = c // 4, (c % 4) * T
        m = dict(common)
        m["x_sl"] = np.ascontiguousarray(x[b, t0:t0 + T, :])
        in_maps.append(m)
    return in_maps


def kernel(**inputs):
    in_maps = make_inputs(inputs)
    nc, _ = build(dbg=False, stage=4)
    res = run_bass_kernel_spmd(nc, in_maps, core_ids=list(range(8)))
    out = np.empty((2, S, D), dtype=np.float32)
    for c in range(8):
        b, t0 = c // 4, (c % 4) * T
        out[b, t0:t0 + T, :] = np.asarray(res.results[c]["out_sl"])
    return out


# version 02
# speedup vs baseline: 1.1230x; 1.1230x over previous
import contextlib
import math
import numpy as np
import concourse.bass as bass
import concourse.mybir as mybir
from concourse.bass_utils import run_bass_kernel_spmd

F32 = mybir.dt.float32
AX = mybir.AxisListType.X
OP = mybir.AluOpType
ACT = mybir.ActivationFunctionType

T = 2048
D = 4096
KVD = 1024
NH = 8
HD = 128
S = 8192
E = 8
R = 32
INV_SQRT_HD = 1.0 / math.sqrt(HD)

ENGS = ("sync", "tensor", "vector", "scalar", "gpsimd")


class Sem:
    def __init__(self, handle):
        self.h = handle
        self.n = 0


class Prog:
    def __init__(self):
        self.q = {e: [] for e in ENGS}

    def em(self, eng, f):
        self.q[eng].append(f)

    def wait(self, eng, sem, val=None):
        v = sem.n if val is None else val
        if v <= 0:
            return
        self.em(eng, lambda e, s=sem.h, v=v: e.wait_ge(s, v))

    def dma(self, eng, out, in_, sem):
        sem.n += 16
        self.em(eng, lambda e, o=out, i=in_, s=sem.h: e.dma_start(out=o, in_=i).then_inc(s, 16))

    def op(self, eng, fn, sem=None, inc=1):
        if sem is None:
            self.em(eng, fn)
        else:
            sem.n += inc
            self.em(eng, lambda e, f=fn, s=sem.h, k=inc: f(e).then_inc(s, k))

    def vop(self, fn, sem=None):
        self.op("vector", fn, sem)
        self.em("vector", lambda e: e.drain())


def build(dbg: bool = False, stage: int = 4):
    nc = bass.Bass("TRN2", target_bir_lowering=False)

    x_sl = nc.dram_tensor("x_sl", [T, D], F32, kind="ExternalInput").ap()
    wq = nc.dram_tensor("wq", [D, D], F32, kind="ExternalInput").ap()
    wk = nc.dram_tensor("wk", [D, KVD], F32, kind="ExternalInput").ap()
    wv = nc.dram_tensor("wv", [D, KVD], F32, kind="ExternalInput").ap()
    wo = nc.dram_tensor("wo", [KVD, D], F32, kind="ExternalInput").ap()
    wT = nc.dram_tensor("wT", [D, D], F32, kind="ExternalInput").ap()
    w8 = nc.dram_tensor("w8", [E, D], F32, kind="ExternalInput").ap()
    bvec = nc.dram_tensor("bvec", [1, D], F32, kind="ExternalInput").ap()
    amat = nc.dram_tensor("amat", [D, R], F32, kind="ExternalInput").ap()
    bmat = nc.dram_tensor("bmat", [R, D], F32, kind="ExternalInput").ap()
    limes = nc.dram_tensor("limes", [E, R], F32, kind="ExternalInput").ap()
    limesT = nc.dram_tensor("limesT", [R, E], F32, kind="ExternalInput").ap()
    identH = nc.dram_tensor("identH", [128, 128], F32, kind="ExternalInput").ap()
    out_sl = nc.dram_tensor("out_sl", [T, D], F32, kind="ExternalOutput").ap()

    xnT_h = nc.dram_tensor("xnT_h", [D, T], F32).ap()
    wqe_h = nc.dram_tensor("wqe_h", [D, KVD], F32).ap()
    qT_h = nc.dram_tensor("qT_h", [KVD, T], F32).ap()
    attnT_h = nc.dram_tensor("attnT_h", [KVD, T], F32).ap()
    cc_in = nc.dram_tensor("cc_in", [2 * KVD, T], F32).ap()
    cc_out = nc.dram_tensor("cc_out", [4 * 2 * KVD, T], F32).ap()
    ar_in = nc.dram_tensor("ar_in", [1, 2], F32).ap()
    ar_out = nc.dram_tensor("ar_out", [1, 2], F32).ap()

    dbg_outs = {}
    if dbg:
        for nm, shp in [("d_row", [1, 64]), ("d_sm", [128, 4]),
                        ("d_xnT", [D, T]), ("d_zT", [R, T]), ("d_Hs", [128, 64]),
                        ("d_Ds", [128, 64]), ("d_scales", [1, 2]), ("d_qT", [KVD, T]),
                        ("d_kv", [4 * 2 * KVD, T]), ("d_attnT", [KVD, T]),
                        ("d_zpm", [R, T]), ("d_wfT", [E, T])]:
            dbg_outs[nm] = nc.dram_tensor(nm, shp, F32, kind="ExternalOutput").ap()

    def sb(name, shape):
        return nc.alloc_sbuf_tensor(name, shape, F32)

    big0 = sb("big0", [128, 16384])
    big1 = sb("big1", [128, 16448])
    qtb = sb("qtb", [128, 4096])
    aring = sb("aring", [128, 2048])
    zT = sb("zT", [R, T])
    zpm = sb("zpm", [R, T])
    ident = sb("ident", [128, 128])
    ones1 = sb("ones1", [1, 128])
    negones1 = sb("negones1", [1, 128])
    ones_col = sb("ones_col", [128, 1])
    b_sb = sb("b_sb", [1, D])
    bm_t = [sb("bm_t0", [R, 512]), sb("bm_t1", [R, 512])]
    limes_sb = sb("limes_sb", [E, R])
    limesh_sb = sb("limesh_sb", [E, R])
    limesT_sb = sb("limesT_sb", [R, E])
    amat_sb = sb("amat_sb", [128, 32 * R])
    w8t_sb = sb("w8t_sb", [128, 32 * E])
    Hs = sb("Hs", [128, 64])
    Ds = sb("Ds", [128, 64])
    stats = sb("stats", [128, 32])
    small = sb("small", [128, 32])
    row = sb("row", [1, 64])
    sq0 = sb("sq0", [128, 256])
    sq1 = sb("sq1", [128, 256])
    negC = sb("negC", [128, 1])
    astage = sb("astage", [128, 128])
    astage2 = sb("astage2", [128, 128])
    gat = sb("gat", [128, 448])
    scal2 = sb("scal2", [1, 2])

    wfT = aring.ap()[0:E, 0:T]

    pk = [nc.alloc_psum_tensor(f"pk{i}", [128, 512], F32) for i in range(8)]

    es = contextlib.ExitStack()
    LD = Sem(es.enter_context(nc.semaphore("LD")))
    ST = Sem(es.enter_context(nc.semaphore("ST")))
    TT = Sem(es.enter_context(nc.semaphore("TT")))
    VV = Sem(es.enter_context(nc.semaphore("VV")))
    SS = Sem(es.enter_context(nc.semaphore("SS")))
    CC = Sem(es.enter_context(nc.semaphore("CC")))
    SC = Sem(es.enter_context(nc.semaphore("SC")))
    EX = Sem(es.enter_context(nc.semaphore("EX")))
    AVK = Sem(es.enter_context(nc.semaphore("AVK")))

    p = Prog()

    def fin():
        p.wait("gpsimd", ST)
        with nc.Block() as block:
            @block.sync
            def _(e):
                for f in p.q["sync"]:
                    f(e)

            @block.tensor
            def _(e):
                for f in p.q["tensor"]:
                    f(e)

            @block.vector
            def _(e):
                for f in p.q["vector"]:
                    f(e)

            @block.scalar
            def _(e):
                for f in p.q["scalar"]:
                    f(e)

            @block.gpsimd
            def _(e):
                for f in p.q["gpsimd"]:
                    f(e)
        es.close()
        return nc, dbg_outs

    p.op("vector", lambda e: e.memset(ones1[:, :], 1.0))
    p.op("vector", lambda e: e.memset(negones1[:, :], -1.0))
    p.op("vector", lambda e: e.memset(ones_col[:, :], 1.0))
    w8_sb = big0[0:8, 0:4096]
    for dst, src in [(ident[:, :], identH), (b_sb[:, :], bvec),
                     (limes_sb[:, :], limes), (limesT_sb[:, :], limesT), (w8_sb, w8)]:
        p.dma("sync", dst, src, LD)
    p.dma("sync", amat_sb.ap().rearrange("p (c r) -> p c r", c=32, r=R),
          amat.rearrange("(c p) r -> p c r", p=128), LD)
    setup_ld = LD.n
    p.wait("vector", LD, setup_ld)
    p.op("vector", lambda e: e.tensor_scalar_mul(limesh_sb[:, :], limes_sb[:, :], 0.5), VV)
    p.wait("tensor", LD, setup_ld)
    w8t_vv = []
    for d in range(32):
        if d >= 2:
            p.wait("tensor", VV, w8t_vv[d - 2])
        p.op("tensor", lambda e, d=d: e.transpose(
            pk[6 + d % 2][0:128, 0:8], w8_sb[0:8, 128 * d:128 * d + 128], ident[0:8, 0:8]), TT)
        p.wait("vector", TT)
        p.op("vector", lambda e, d=d: e.tensor_copy(w8t_sb[:, 8 * d:8 * d + 8], pk[6 + d % 2][0:128, 0:8]), VV)
        w8t_vv.append(VV.n)
    p.wait("tensor", VV)
    w8_done_tt = TT.n

    x3 = x_sl.rearrange("(i h p) d -> i p h d", h=2, p=128)
    fold_vv = []
    fold_st = []
    xn_ss = []
    tr_tt = []
    st_after_tile = []
    for i in range(8):
        xb = big0[:, 8192 * (i % 2):8192 * (i % 2) + 8192]
        if i == 0:
            p.wait("sync", TT, w8_done_tt)
        if i >= 2:
            p.wait("sync", SS, xn_ss[i - 2])
        p.dma("sync", xb.rearrange("p (h d) -> p h d", h=2, d=D), x3[i], LD)
        ld_tile = LD.n
        st = stats
        p.wait("vector", LD, ld_tile)
        p.op("vector", lambda e, xb=xb: e.reduce_sum(
            st[:, 0:2], xb.rearrange("p (h d) -> p h d", h=2, d=D), AX), VV)
        p.wait("scalar", LD, ld_tile)
        if i >= 1:
            p.wait("scalar", TT, tr_tt[i - 1])
            p.wait("scalar", ST, st_after_tile[i - 1])
        for h in range(2):
            p.op("scalar", lambda e, xb=xb, h=h: e.activation(
                big1[:, 8192 + 4096 * h:8192 + 4096 * h + 4096],
                xb[:, 4096 * h:4096 * h + 4096], ACT.Square, accum_out=st[:, 2 + h:3 + h]), SS)
        p.wait("vector", SS)
        p.em("vector", lambda e: e.drain())
        p.vop(lambda e: e.tensor_scalar_mul(st[:, 0:2], st[:, 0:2], 1.0 / D))
        p.vop(lambda e: e.tensor_scalar_mul(st[:, 2:4], st[:, 2:4], 1.0 / D))
        p.vop(lambda e: e.tensor_tensor(st[:, 4:6], st[:, 0:2], st[:, 0:2], OP.mult))
        p.vop(lambda e: e.tensor_sub(st[:, 4:6], st[:, 2:4], st[:, 4:6]))
        p.vop(lambda e: e.tensor_scalar_add(st[:, 10:12], st[:, 4:6], 1e-5), VV)
        p.wait("scalar", VV)
        p.op("scalar", lambda e: e.activation(st[:, 12:14], st[:, 10:12], ACT.Sqrt), SS)
        p.wait("vector", SS)
        p.vop(lambda e: e.reciprocal(st[:, 6:8], st[:, 12:14]))
        p.vop(lambda e: e.tensor_tensor(st[:, 14:16], st[:, 6:8], st[:, 6:8], OP.mult))
        p.vop(lambda e: e.tensor_tensor(st[:, 14:16], st[:, 14:16], st[:, 10:12], OP.mult))
        p.vop(lambda e: e.tensor_scalar(st[:, 14:16], st[:, 14:16], -0.5, 1.5, OP.mult, OP.add))
        p.vop(lambda e: e.tensor_tensor(st[:, 6:8], st[:, 6:8], st[:, 14:16], OP.mult))
        p.vop(lambda e: e.tensor_tensor(st[:, 8:10], st[:, 0:2], st[:, 6:8], OP.mult))
        p.vop(lambda e: e.tensor_scalar_mul(st[:, 8:10], st[:, 8:10], -1.0), VV)
        p.wait("scalar", VV)
        for h in range(2):
            p.op("scalar", lambda e, xb=xb, h=h: e.activation(
                big1[:, 4096 * h:4096 * h + 4096], xb[:, 4096 * h:4096 * h + 4096], ACT.Identity,
                bias=st[:, 8 + h:9 + h], scale=st[:, 6 + h:7 + h]), SS)
        xn_ss.append(SS.n)
        p.wait("tensor", SS, xn_ss[-1])
        if i >= 1:
            p.wait("tensor", ST, st_after_tile[i - 1])
        evac_vv = []
        for c in range(64):
            bank = 4 + (c % 4)
            if c >= 4:
                p.wait("tensor", VV, evac_vv[c - 4])
            p.op("tensor", lambda e, c=c, bank=bank: e.transpose(
                pk[bank][:, 0:128], big1[:, 128 * c:128 * c + 128], ident[:, :]), TT)
            p.wait("vector", TT)
            h_, d_ = c // 32, c % 32
            p.op("vector", lambda e, h_=h_, d_=d_, bank=bank: e.tensor_copy(
                big1[:, 8192 + 256 * d_ + 128 * h_:8192 + 256 * d_ + 128 * h_ + 128],
                pk[bank][:, 0:128]), VV)
            evac_vv.append(VV.n)
        tr_tt.append(TT.n)
        xnT_tile = big1[:, 8192:16384]
        p.wait("tensor", VV, evac_vv[-1])
        for d in range(32):
            p.op("tensor", lambda e, d=d: e.matmul(
                pk[0][0:R, 0:256], amat_sb[:, 32 * d:32 * d + 32],
                xnT_tile[:, 256 * d:256 * d + 256], start=(d == 0), stop=(d == 31)),
                TT if d == 31 else None)
        for d in range(32):
            p.op("tensor", lambda e, d=d: e.matmul(
                pk[1][0:128, 0:8], xnT_tile[:, 256 * d + 1:256 * d + 256:2],
                w8t_sb[:, 8 * d:8 * d + 8], start=(d == 0), stop=(d == 31)),
                TT if d == 31 else None)
        p.wait("vector", TT)
        p.op("vector", lambda e, i=i: e.tensor_copy(zT[:, 256 * i:256 * i + 256], pk[0][0:R, 0:256]), VV)
        p.op("vector", lambda e, i=i: e.tensor_copy(Hs[:, 8 * i:8 * i + 8], pk[1][0:128, 0:8]), VV)
        p.wait("gpsimd", VV)
        for gq in range(4):
            src = xnT_tile[:, 2048 * gq:2048 * gq + 2048].rearrange("p (c j) -> p c j", c=8, j=256)
            dst = xnT_h.rearrange("(c p) t -> p c t", p=128)[:, 8 * gq:8 * gq + 8, 256 * i:256 * i + 256]
            p.dma("gpsimd", dst, src, ST)
        st_after_tile.append(ST.n)
        p.wait("tensor", VV)
        for fc in range(8 * i, 8 * i + 8):
            d, hf = fc // 2, fc % 2
            ftmp = qtb[:, 2048 * (fc % 2):2048 * (fc % 2) + 2048]
            fout = aring[:, 1024 * (fc % 2):1024 * (fc % 2) + 512]
            if fc >= 2:
                p.wait("sync", VV, fold_vv[fc - 2])
            p.dma("sync", ftmp, wq[128 * d:128 * d + 128, 2048 * hf:2048 * hf + 2048], LD)
            p.wait("vector", LD)
            if fc >= 2:
                p.wait("vector", ST, fold_st[fc - 2])
            o4 = fout.rearrange("p (kv j) -> p kv j", kv=4, j=128)
            v8 = ftmp.rearrange("p (kv g j) -> p kv g j", kv=4, g=4, j=128)
            p.op("vector", lambda e, v8=v8, o4=o4: e.tensor_tensor(o4, v8[:, :, 0, :], v8[:, :, 1, :], OP.add))
            p.op("vector", lambda e, v8=v8, o4=o4: e.tensor_tensor(o4, o4, v8[:, :, 2, :], OP.add))
            p.op("vector", lambda e, v8=v8, o4=o4: e.tensor_tensor(o4, o4, v8[:, :, 3, :], OP.add), VV)
            fold_vv.append(VV.n)
            p.wait("gpsimd", VV)
            p.dma("gpsimd", wqe_h[128 * d:128 * d + 128, 512 * hf:512 * hf + 512], fout, ST)
            fold_st.append(ST.n)

    for c in range(8):
        p.op("tensor", lambda e, c=c: e.matmul(
            pk[2][0:128, 0:8], zT[:, 256 * c + 1:256 * c + 256:2], limesT_sb[:, :],
            start=True, stop=True), TT)
        p.wait("vector", TT)
        p.op("vector", lambda e, c=c: e.tensor_copy(Ds[:, 8 * c:8 * c + 8], pk[2][0:128, 0:8]), VV)
        p.wait("tensor", VV)
    for j, buf in ((0, Hs), (1, Ds)):
        p.vop(lambda e, buf=buf: e.reduce_max(
            small[:, 0:1], buf[:, :], AX, apply_absolute_value=True), VV)
        p.wait("tensor", VV)
        p.op("tensor", lambda e: e.transpose(pk[3][0:1, 0:128], small[:, 0:1], ident[:, :]), TT)
        p.wait("vector", TT)
        p.vop(lambda e, j=j: e.reduce_max(row[0:1, j:j + 1], pk[3][0:1, 0:128], AX))
        p.vop(lambda e, j=j: e.tensor_scalar_max(row[0:1, j:j + 1], row[0:1, j:j + 1], 1e-6), VV)
        p.wait("tensor", VV)
    p.wait("gpsimd", VV)
    p.dma("gpsimd", ar_in, row[0:1, 0:2], ST)

    if dbg:
        p.wait("gpsimd", VV)
        p.dma("gpsimd", dbg_outs["d_zT"], zT[:, :], ST)
        p.dma("gpsimd", dbg_outs["d_Hs"], Hs[:, :], ST)
        p.dma("gpsimd", dbg_outs["d_Ds"], Ds[:, :], ST)
        p.dma("gpsimd", dbg_outs["d_xnT"], xnT_h, ST)
        p.wait("gpsimd", ST)
        p.wait("gpsimd", TT, TT.n)
        p.wait("gpsimd", VV, VV.n)
        p.wait("gpsimd", SS, SS.n)
        p.op("gpsimd", lambda e: e.collective_compute(
            "AllReduce", OP.max, replica_groups=[[0, 1, 2, 3, 4, 5, 6, 7]],
            ins=[ar_in.opt()], outs=[ar_out.opt()]), CC)
        p.wait("gpsimd", CC)
        p.dma("gpsimd", dbg_outs["d_scales"], ar_out, ST)
        p.dma("gpsimd", dbg_outs["d_row"], row[0:1, 0:64], ST)
        p.dma("gpsimd", dbg_outs["d_sm"], small[:, 0:4], ST)
    if stage <= 1:
        return fin()

    xnT_src = xnT_h.rearrange("(c p) t -> p c t", p=128)
    slice_tt = []
    evac_hist = []
    stg_st = []

    p.wait("sync", SS, xn_ss[-1])
    p.wait("sync", ST, st_after_tile[-1])
    p.wait("sync", TT, tr_tt[-1])
    p.wait("vector", ST, ST.n)

    def emit_proj(weight_ap, dst_ap):
        for half in range(2):
            src = weight_ap[:, 512 * half:512 * half + 512].rearrange("(c p) o -> p c o", p=128)
            dstw = big1[:, 0:16384].rearrange("p (c o) -> p c o", c=32, o=512)
            if slice_tt:
                p.wait("sync", TT, slice_tt[-1])
            for gq in range(4):
                p.dma("sync", dstw[:, 8 * gq:8 * gq + 8, :], src[:, 8 * gq:8 * gq + 8, :], LD)
            p.wait("tensor", LD, LD.n)
            for tsl in range(8):
                xb = big0[:, 8192 * (tsl % 2):8192 * (tsl % 2) + 8192]
                if len(slice_tt) >= 2:
                    p.wait("sync", TT, slice_tt[-2])
                xdst = xb.rearrange("p (c j) -> p c j", c=32, j=256)
                for gq in range(4):
                    p.dma("sync", xdst[:, 8 * gq:8 * gq + 8, :],
                          xnT_src[:, 8 * gq:8 * gq + 8, 256 * tsl:256 * tsl + 256], LD)
                sld = LD.n
                p.wait("tensor", LD, sld)
                for oc in range(4):
                    bank = oc % 2
                    if len(evac_hist) >= 2:
                        p.wait("tensor", VV, evac_hist[-2])
                    for d in range(32):
                        p.op("tensor", lambda e, d=d, oc=oc, bank=bank, xb=xb: e.matmul(
                            pk[bank][:, 0:256], big1[:, 512 * d + 128 * oc:512 * d + 128 * oc + 128],
                            xb[:, 256 * d:256 * d + 256], start=(d == 0), stop=(d == 31)),
                            TT if d == 31 else None)
                    p.wait("vector", TT)
                    stg = aring[:, 512 * (oc % 2):512 * (oc % 2) + 256]
                    if len(stg_st) >= 2:
                        p.wait("vector", ST, stg_st[-2])
                    p.op("vector", lambda e, stg=stg, bank=bank: e.tensor_copy(stg, pk[bank][:, 0:256]), VV)
                    evac_hist.append(VV.n)
                    p.wait("gpsimd", VV)
                    orow = 512 * half + 128 * oc
                    p.dma("gpsimd", dst_ap[orow:orow + 128, 256 * tsl:256 * tsl + 256], stg, ST)
                    stg_st.append(ST.n)
                slice_tt.append(TT.n)

    emit_proj(wk, cc_in[0:KVD, :])
    emit_proj(wv, cc_in[KVD:2 * KVD, :])
    emit_proj(wqe_h, qT_h)

    p.wait("gpsimd", ST)
    p.wait("gpsimd", TT, TT.n)
    p.wait("gpsimd", VV, VV.n)
    p.wait("gpsimd", SS, SS.n)
    p.wait("gpsimd", LD, LD.n)
    p.op("gpsimd", lambda e: e.collective_compute(
        "AllReduce", OP.max, replica_groups=[[0, 1, 2, 3, 4, 5, 6, 7]],
        ins=[ar_in.opt()], outs=[ar_out.opt()]), CC)
    p.wait("gpsimd", CC)
    p.op("gpsimd", lambda e: e.collective_compute(
        "AllGather", OP.bypass, replica_groups=[[0, 1, 2, 3], [4, 5, 6, 7]],
        ins=[cc_in.opt()], outs=[cc_out.opt()]), CC)

    if dbg:
        p.wait("gpsimd", ST)
        p.wait("gpsimd", CC)
        p.dma("gpsimd", dbg_outs["d_qT"], qT_h, ST)
        p.dma("gpsimd", dbg_outs["d_kv"], cc_out, ST)
    if stage <= 2:
        return fin()

    p.wait("sync", CC, CC.n)
    p.wait("sync", TT, slice_tt[-1])
    vones = big1[:, 8192:16448]
    vtraw = big1[:, 0:8192]
    p.wait("vector", TT, slice_tt[-1])
    p.op("vector", lambda e: e.memset(
        vones.rearrange("p (c j) -> p c j", c=64, j=129)[:, :, 128:129], 1.0), VV)
    vones_init_vv = VV.n

    kt_consumed = [0] * NH
    vt_consumed = [0] * NH
    av_done = [0] * NH
    norm_consumed = [0] * NH

    for h in range(NH):
        ktb = big0[:, 8192 * (h % 2):8192 * (h % 2) + 8192]
        qb = qtb[:, 2048 * (h % 2):2048 * (h % 2) + 2048]
        if h >= 2:
            p.wait("sync", TT, kt_consumed[h - 2])
            p.wait("sync", SS, norm_consumed[h - 2])
        for r_ in range(4):
            p.dma("sync", ktb[:, 2048 * r_:2048 * r_ + 2048],
                  cc_out[2048 * r_ + HD * h:2048 * r_ + HD * h + HD, :], LD)
        if h >= 1:
            p.wait("sync", TT, vt_consumed[h - 1])
        for r_ in range(4):
            p.dma("sync", vtraw[:, 2048 * r_:2048 * r_ + 2048],
                  cc_out[2048 * r_ + KVD + HD * h:2048 * r_ + KVD + HD * h + HD, :], LD)
        p.dma("sync", qb, qT_h[HD * h:HD * h + HD, :], LD)
        ld_head = LD.n

        p.wait("tensor", LD, ld_head)
        if h >= 1:
            p.wait("tensor", AVK, av_done[h - 1])
        p.wait("tensor", VV)
        evac_vv = []
        for c in range(64):
            bank = 6 + (c % 2)
            if c >= 2:
                p.wait("tensor", VV, evac_vv[c - 2])
            p.op("tensor", lambda e, c=c, bank=bank: e.transpose(
                pk[bank][:, 0:128], vtraw[:, 128 * c:128 * c + 128], ident[:, :]), TT)
            p.wait("vector", TT)
            p.op("vector", lambda e, c=c, bank=bank: e.tensor_copy(
                vones[:, 129 * c:129 * c + 128], pk[bank][:, 0:128]), VV)
            evac_vv.append(VV.n)
        vt_consumed[h] = TT.n
        p.wait("tensor", VV, evac_vv[-1])

        p.vop(lambda e: e.memset(row[0:1, 2:4], 0.0), VV)
        p.wait("scalar", LD, ld_head)
        p.wait("scalar", VV)
        mm_tt = []
        for which, buf, ntile, slot in (("k", ktb, 32, 2), ("q", qb, 8, 3)):
            for t in range(ntile):
                sqb = sq0 if t % 2 == 0 else sq1
                if len(mm_tt) >= 2:
                    p.wait("scalar", TT, mm_tt[-2])
                p.op("scalar", lambda e, buf=buf, t=t, sqb=sqb: e.activation(
                    sqb[:, :], buf[:, 256 * t:256 * t + 256], ACT.Square), SS)
                p.wait("tensor", SS)
                p.wait("tensor", VV)
                p.op("tensor", lambda e, sqb=sqb: e.matmul(
                    pk[5][0:1, 0:256], ones_col[:, :], sqb[:, :], start=True, stop=True), TT)
                mm_tt.append(TT.n)
                p.wait("vector", TT)
                p.vop(lambda e: e.reduce_max(row[0:1, 4:5], pk[5][0:1, 0:256], AX))
                p.vop(lambda e, slot=slot: e.tensor_tensor(
                    row[0:1, slot:slot + 1], row[0:1, slot:slot + 1], row[0:1, 4:5], OP.max), VV)
        norm_consumed[h] = SS.n
        p.vop(lambda e: e.tensor_tensor(row[0:1, 5:6], row[0:1, 2:3], row[0:1, 3:4], OP.mult), VV)
        p.wait("scalar", VV)
        p.op("scalar", lambda e: e.activation(row[0:1, 6:7], row[0:1, 5:6], ACT.Sqrt), SS)
        p.wait("vector", SS)
        p.vop(lambda e: e.tensor_scalar_mul(row[0:1, 6:7], row[0:1, 6:7], 1.01 * INV_SQRT_HD), VV)
        p.wait("tensor", VV)
        p.op("tensor", lambda e: e.matmul(pk[5][0:128, 0:1], negones1[:, :], row[0:1, 6:7],
                                          start=True, stop=True), TT)
        p.wait("vector", TT)
        p.wait("vector", EX, EX.n)
        p.op("vector", lambda e: e.tensor_copy(negC[:, :], pk[5][0:128, 0:1]), VV)
        negc_vv = VV.n
        p.wait("scalar", VV, negc_vv)
        p.wait("tensor", VV, negc_vv)

        for qt in range(4):
            sc_base = SC.n
            ex_base = EX.n
            avk_base = AVK.n
            epi_vv = VV.n
            for step in range(65):
                if step < 64:
                    kc = step
                    sbank = 4 + (kc % 3)
                    if kc >= 3:
                        p.wait("tensor", EX, ex_base + kc - 2)
                    p.op("tensor", lambda e, kc=kc, qt=qt, sbank=sbank, ktb=ktb, qb=qb: e.matmul(
                        pk[sbank][:, 0:512], ktb[:, 128 * kc:128 * kc + 128],
                        qb[:, 512 * qt:512 * qt + 512], start=True, stop=True), SC)
                    p.wait("scalar", SC, sc_base + kc + 1)
                    if kc >= 4:
                        p.wait("scalar", AVK, avk_base + kc - 3)
                    p.op("scalar", lambda e, kc=kc, sbank=sbank: e.activation(
                        aring[:, 512 * (kc % 4):512 * (kc % 4) + 512], pk[sbank][:, 0:512],
                        ACT.Exp, bias=negC[:, 0:1], scale=INV_SQRT_HD), EX)
                if step >= 1:
                    j = step - 1
                    p.wait("tensor", EX, ex_base + j + 1)
                    for qs in range(4):
                        p.op("tensor", lambda e, j=j, qs=qs: e.matmul(
                            pk[qs][:, 0:129],
                            aring[:, 512 * (j % 4) + 128 * qs:512 * (j % 4) + 128 * qs + 128],
                            vones[:, 129 * j:129 * j + 129], start=(j == 0), stop=(j == 63)),
                            AVK if qs == 3 else None)
            p.wait("vector", AVK, AVK.n)
            for qs in range(4):
                p.vop(lambda e, qs=qs: e.tensor_copy(small[:, 8:9], pk[qs][:, 128:129]))
                p.vop(lambda e: e.reciprocal(small[:, 9:10], small[:, 8:9]))
                p.op("vector", lambda e, qs=qs: e.tensor_scalar(
                    astage[:, :], pk[qs][:, 0:128], small[0:128, 9:10], None, OP.mult), VV)
                p.wait("tensor", VV)
                p.op("tensor", lambda e: e.transpose(pk[7][:, 0:128], astage[:, :], ident[:, :]), TT)
                p.wait("vector", TT)
                p.wait("vector", ST)
                p.op("vector", lambda e: e.tensor_copy(astage2[:, :], pk[7][:, 0:128]), VV)
                p.wait("gpsimd", VV)
                p.dma("gpsimd", attnT_h[HD * h:HD * h + HD,
                                        512 * qt + 128 * qs:512 * qt + 128 * qs + 128], astage2[:, :], ST)
            p.wait("tensor", VV)
        kt_consumed[h] = TT.n
        av_done[h] = AVK.n

    if dbg:
        p.wait("gpsimd", ST)
        p.dma("gpsimd", dbg_outs["d_attnT"], attnT_h, ST)
    if stage <= 3:
        return fin()

    p.wait("gpsimd", CC, CC.n)
    p.dma("gpsimd", scal2[0:1, 0:2], ar_out, LD)
    gld = LD.n
    p.wait("vector", LD, gld)
    p.vop(lambda e: e.reciprocal(row[0:1, 8:10], scal2[0:1, 0:2]))
    p.vop(lambda e: e.tensor_scalar_mul(row[0:1, 8:10], row[0:1, 8:10], 0.5), VV)
    for j in range(2):
        p.wait("tensor", VV)
        p.op("tensor", lambda e, j=j: e.matmul(pk[7][0:128, 0:1], ones1[:, :], row[0:1, 8 + j:9 + j],
                                               start=True, stop=True), TT)
        p.wait("vector", TT)
        p.vop(lambda e, j=j: e.tensor_copy(small[:, 10 + j:11 + j], pk[7][0:128, 0:1]), VV)
        p.wait("tensor", VV)
    L = gat[:, 0:64]
    P_ = gat[:, 64:128]
    p.vop(lambda e: e.tensor_scalar(L, Hs[:, :], small[0:128, 10:11], None, OP.mult))
    p.vop(lambda e: e.scalar_tensor_tensor(L, Ds[:, :], small[0:128, 11:12], L, OP.mult, OP.add), VV)
    p.wait("scalar", VV)
    p.op("scalar", lambda e: e.activation(P_, L, ACT.Exp), SS)
    p.wait("vector", SS)
    g3 = P_.rearrange("p (c e) -> p c e", c=8, e=E)
    p.vop(lambda e: e.reduce_sum(small[:, 12:20], g3, AX))
    p.vop(lambda e: e.reciprocal(small[:, 20:28], small[:, 12:20]))
    for c in range(8):
        p.vop(lambda e, c=c: e.tensor_scalar(
            P_[:, 8 * c:8 * c + 8], P_[:, 8 * c:8 * c + 8], small[0:128, 20 + c:21 + c], None, OP.mult))
    for c in range(8):
        p.vop(lambda e, c=c: e.max(out=gat[:, 192 + 8 * c:192 + 8 * c + 8],
                                   in_=P_[:, 8 * c:8 * c + 8]))
    for c in range(8):
        p.vop(lambda e, c=c: e.tensor_scalar(
            gat[:, 256 + 8 * c:256 + 8 * c + 8], P_[:, 8 * c:8 * c + 8],
            gat[0:128, 192 + 8 * c + 1:192 + 8 * c + 2], None, OP.subtract), VV)
    p.wait("scalar", VV)
    p.op("scalar", lambda e: e.activation(gat[:, 320:384], gat[:, 256:320], ACT.Sigmoid, scale=2.0), SS)
    p.wait("vector", SS)
    W_ = gat[:, 384:448]
    p.vop(lambda e: e.tensor_tensor(W_, P_, gat[:, 320:384], OP.mult))
    w3 = W_.rearrange("p (c e) -> p c e", c=8, e=E)
    p.vop(lambda e: e.reduce_sum(small[:, 12:20], w3, AX))
    p.vop(lambda e: e.tensor_scalar_add(small[:, 12:20], small[:, 12:20], 1e-9))
    p.vop(lambda e: e.reciprocal(small[:, 20:28], small[:, 12:20]))
    for c in range(8):
        p.vop(lambda e, c=c: e.tensor_scalar(
            W_[:, 8 * c:8 * c + 8], W_[:, 8 * c:8 * c + 8], small[0:128, 20 + c:21 + c], None, OP.mult), VV)
    for c in range(8):
        p.wait("tensor", VV)
        p.op("tensor", lambda e, c=c: e.transpose(pk[7][0:8, 0:128], W_[:, 8 * c:8 * c + 8], ident[:, :]), TT)
        p.wait("vector", TT)
        p.op("vector", lambda e, c=c: e.tensor_copy(wfT[0:8, 256 * c:256 * c + 256:2], pk[7][0:8, 0:128]))
        p.op("vector", lambda e, c=c: e.tensor_copy(wfT[0:8, 256 * c + 1:256 * c + 256:2], pk[7][0:8, 0:128]), VV)
    p.wait("tensor", VV)
    for c in range(4):
        p.op("tensor", lambda e, c=c: e.matmul(
            pk[3][0:R, 0:512], limesh_sb[:, :], wfT[0:8, 512 * c:512 * c + 512],
            start=True, stop=True), TT)
        p.wait("vector", TT)
        p.op("vector", lambda e, c=c: e.tensor_tensor(
            zpm[:, 512 * c:512 * c + 512], pk[3][0:R, 0:512], zT[:, 512 * c:512 * c + 512], OP.mult), VV)
        p.wait("tensor", VV)

    if dbg:
        p.wait("gpsimd", VV)
        p.dma("gpsimd", dbg_outs["d_zpm"], zpm[:, :], ST)
        p.dma("gpsimd", dbg_outs["d_wfT"], wfT[0:E, 0:T], ST)

    wo_src = wo.rearrange("(c p) o -> p c o", p=128)
    wT_src = wT.rearrange("(c p) o -> p c o", p=128)
    attnT_src = attnT_h.rearrange("(c p) t -> p c t", p=128)

    p.wait("sync", TT, TT.n)
    p.wait("sync", AVK, AVK.n)
    p.wait("sync", VV, VV.n)
    p.wait("tensor", VV, VV.n)

    blk_tt = []
    hd_tt = []
    smm_tt = []
    wt_hist = []
    silu_ss = []
    sg_vv = []
    xst_vv = []

    for tb in range(4):
        if tb >= 1:
            p.wait("sync", TT, hd_tt[-1])
        for gq in range(4):
            p.dma("sync", big0.ap().rearrange("p (c j) -> p c j", c=32, j=512)[:, 8 * gq:8 * gq + 8, :],
                  xnT_src[:, 8 * gq:8 * gq + 8, 512 * tb:512 * tb + 512], LD)
        if tb >= 1:
            p.wait("scalar", TT, smm_tt[-1])
        p.dma("scalar", qtb.ap().rearrange("p (c j) -> p c j", c=8, j=512),
              attnT_src[:, :, 512 * tb:512 * tb + 512], LD)
        blk_ld = LD.n
        p.wait("tensor", LD, blk_ld)
        for o in range(8):
            wop = big1[:, 4096 * (o % 2):4096 * (o % 2) + 4096]
            if len(smm_tt) >= 2:
                p.wait("scalar", TT, smm_tt[-2])
            p.dma("scalar", wop.rearrange("p (c j) -> p c j", c=8, j=512),
                  wo_src[:, :, 512 * o:512 * o + 512], LD)
            wo_ld = LD.n
            if len(hd_tt) >= 2:
                p.wait("scalar", TT, hd_tt[-2])
            p.dma("scalar", bm_t[o % 2][:, :], bmat[0:R, 512 * o:512 * o + 512], LD)
            bm_ld = LD.n
            for d in range(32):
                wtp = big1[:, 8192 + 512 * (d % 2):8192 + 512 * (d % 2) + 512]
                if len(wt_hist) >= 2:
                    p.wait("sync", TT, wt_hist[-2])
                p.dma("sync", wtp, wT_src[:, d, 512 * o:512 * o + 512], LD)
                p.wait("tensor", LD, LD.n)
                if d == 0 and silu_ss:
                    p.wait("tensor", SS, silu_ss[-1])
                for ts in range(4):
                    p.op("tensor", lambda e, d=d, ts=ts, wtp=wtp: e.matmul(
                        pk[ts][:, 0:512], big0[:, 512 * d + 128 * ts:512 * d + 128 * ts + 128],
                        wtp, start=(d == 0), stop=False), TT if ts == 3 else None)
                wt_hist.append(TT.n)
            for ts in range(4):
                p.op("tensor", lambda e, ts=ts, o=o: e.matmul(
                    pk[ts][:, 0:512], ones1[:, :], b_sb[0:1, 512 * o:512 * o + 512],
                    start=False, stop=False), None)
                if ts == 0:
                    p.wait("tensor", LD, bm_ld)
                p.op("tensor", lambda e, o=o, ts=ts, tb=tb: e.matmul(
                    pk[ts][:, 0:512], zpm[:, 512 * tb + 128 * ts:512 * tb + 128 * ts + 128],
                    bm_t[o % 2][:, :], start=False, stop=True),
                    TT if ts == 3 else None)
            hdone_tt = TT.n
            hd_tt.append(hdone_tt)
            p.wait("scalar", TT, hdone_tt)
            if sg_vv:
                p.wait("scalar", VV, sg_vv[-1])
            for ts in range(4):
                p.op("scalar", lambda e, ts=ts: e.activation(
                    big1[:, 9216 + 512 * ts:9216 + 512 * ts + 512], pk[ts][:, 0:512], ACT.Silu), SS)
            silu_ss.append(SS.n)
            p.wait("tensor", LD, wo_ld)
            if sg_vv:
                p.wait("tensor", VV, sg_vv[-1])
            for dv in range(8):
                for ts in range(4):
                    p.op("tensor", lambda e, dv=dv, ts=ts, wop=wop: e.matmul(
                        pk[4 + ts][:, 0:512], qtb[:, 512 * dv + 128 * ts:512 * dv + 128 * ts + 128],
                        wop[:, 512 * dv:512 * dv + 512], start=(dv == 0), stop=(dv == 7)),
                        TT if (dv == 7 and ts == 3) else None)
            smm_tt.append(TT.n)
            for ts in range(4):
                xst = aring[:, 512 * (ts % 2):512 * (ts % 2) + 512]
                if len(xst_vv) >= 2:
                    p.wait("scalar", VV, xst_vv[-2])
                p.dma("scalar", xst, x_sl[512 * tb + 128 * ts:512 * tb + 128 * ts + 128,
                                          512 * o:512 * o + 512], LD)
                xl = LD.n
                if ts == 0:
                    p.wait("vector", TT, smm_tt[-1])
                    p.wait("vector", SS, silu_ss[-1])
                p.wait("vector", LD, xl)
                ost = aring[:, 1024 + 512 * (ts % 2):1024 + 512 * (ts % 2) + 512]
                if ts >= 2:
                    p.wait("vector", ST)
                p.op("vector", lambda e, ts=ts, ost=ost: e.tensor_tensor(
                    ost, pk[4 + ts][:, 0:512], big1[:, 9216 + 512 * ts:9216 + 512 * ts + 512], OP.mult))
                p.op("vector", lambda e, ts=ts, ost=ost, xst=xst: e.tensor_tensor(ost, ost, xst, OP.add), VV)
                xst_vv.append(VV.n)
                p.wait("gpsimd", VV)
                p.dma("gpsimd", out_sl[512 * tb + 128 * ts:512 * tb + 128 * ts + 128,
                                       512 * o:512 * o + 512], ost, ST)
            sg_vv.append(VV.n)
        blk_tt.append(TT.n)

    return fin()


def make_inputs(inputs):
    x = np.ascontiguousarray(np.asarray(inputs["L0_attn_x"], dtype=np.float32))
    W = np.asarray(inputs["L0_ffn_W"], dtype=np.float32)
    common = {
        "wq": np.ascontiguousarray(np.asarray(inputs["L0_attn_wq"], dtype=np.float32)),
        "wk": np.ascontiguousarray(np.asarray(inputs["L0_attn_wk"], dtype=np.float32)),
        "wv": np.ascontiguousarray(np.asarray(inputs["L0_attn_wv"], dtype=np.float32)),
        "wo": np.ascontiguousarray(np.asarray(inputs["L0_attn_wo"], dtype=np.float32)),
        "wT": np.ascontiguousarray(W.T),
        "w8": np.ascontiguousarray(W[0:E, :]),
        "bvec": np.ascontiguousarray(np.asarray(inputs["L0_ffn_b"], dtype=np.float32)).reshape(1, D),
        "amat": np.ascontiguousarray(np.asarray(inputs["L0_ffn_A"], dtype=np.float32)),
        "bmat": np.ascontiguousarray(np.asarray(inputs["L0_ffn_Bmat"], dtype=np.float32)),
        "limes": np.ascontiguousarray(np.asarray(inputs["L0_ffn_LiMEs"], dtype=np.float32)),
        "limesT": np.ascontiguousarray(np.asarray(inputs["L0_ffn_LiMEs"], dtype=np.float32).T),
        "identH": np.eye(128, dtype=np.float32),
    }
    in_maps = []
    for c in range(8):
        b, t0 = c // 4, (c % 4) * T
        m = dict(common)
        m["x_sl"] = np.ascontiguousarray(x[b, t0:t0 + T, :])
        in_maps.append(m)
    return in_maps


def kernel(**inputs):
    in_maps = make_inputs(inputs)
    nc, _ = build(dbg=False, stage=4)
    res = run_bass_kernel_spmd(nc, in_maps, core_ids=list(range(8)))
    out = np.empty((2, S, D), dtype=np.float32)
    for c in range(8):
        b, t0 = c // 4, (c % 4) * T
        out[b, t0:t0 + T, :] = np.asarray(res.results[c]["out_sl"])
    return out


# version 03 (final)
# speedup vs baseline: 1.1332x; 1.0091x over previous
import contextlib
import math
import numpy as np
import concourse.bass as bass
import concourse.mybir as mybir
from concourse.bass_utils import run_bass_kernel_spmd

F32 = mybir.dt.float32
AX = mybir.AxisListType.X
OP = mybir.AluOpType
ACT = mybir.ActivationFunctionType

T = 2048
D = 4096
KVD = 1024
NH = 8
HD = 128
S = 8192
E = 8
R = 32
INV_SQRT_HD = 1.0 / math.sqrt(HD)

ENGS = ("sync", "tensor", "vector", "scalar", "gpsimd")


class Sem:
    def __init__(self, handle):
        self.h = handle
        self.n = 0


class Prog:
    def __init__(self):
        self.q = {e: [] for e in ENGS}

    def em(self, eng, f):
        self.q[eng].append(f)

    def wait(self, eng, sem, val=None):
        v = sem.n if val is None else val
        if v <= 0:
            return
        self.em(eng, lambda e, s=sem.h, v=v: e.wait_ge(s, v))

    def dma(self, eng, out, in_, sem):
        sem.n += 16
        self.em(eng, lambda e, o=out, i=in_, s=sem.h: e.dma_start(out=o, in_=i).then_inc(s, 16))

    def op(self, eng, fn, sem=None, inc=1):
        if sem is None:
            self.em(eng, fn)
        else:
            sem.n += inc
            self.em(eng, lambda e, f=fn, s=sem.h, k=inc: f(e).then_inc(s, k))

    def vop(self, fn, sem=None):
        self.op("vector", fn, sem)
        self.em("vector", lambda e: e.drain())


def build(dbg: bool = False, stage: int = 4):
    nc = bass.Bass("TRN2", target_bir_lowering=False)

    x_sl = nc.dram_tensor("x_sl", [T, D], F32, kind="ExternalInput").ap()
    wq = nc.dram_tensor("wq", [D, D], F32, kind="ExternalInput").ap()
    wk = nc.dram_tensor("wk", [D, KVD], F32, kind="ExternalInput").ap()
    wv = nc.dram_tensor("wv", [D, KVD], F32, kind="ExternalInput").ap()
    wo = nc.dram_tensor("wo", [KVD, D], F32, kind="ExternalInput").ap()
    wT = nc.dram_tensor("wT", [D, D], F32, kind="ExternalInput").ap()
    w8 = nc.dram_tensor("w8", [E, D], F32, kind="ExternalInput").ap()
    bvec = nc.dram_tensor("bvec", [1, D], F32, kind="ExternalInput").ap()
    amat = nc.dram_tensor("amat", [D, R], F32, kind="ExternalInput").ap()
    bmat = nc.dram_tensor("bmat", [R, D], F32, kind="ExternalInput").ap()
    limes = nc.dram_tensor("limes", [E, R], F32, kind="ExternalInput").ap()
    limesT = nc.dram_tensor("limesT", [R, E], F32, kind="ExternalInput").ap()
    identH = nc.dram_tensor("identH", [128, 128], F32, kind="ExternalInput").ap()
    out_sl = nc.dram_tensor("out_sl", [T, D], F32, kind="ExternalOutput").ap()

    xnT_h = nc.dram_tensor("xnT_h", [D, T], F32).ap()
    wqe_h = nc.dram_tensor("wqe_h", [D, KVD], F32).ap()
    qT_h = nc.dram_tensor("qT_h", [KVD, T], F32).ap()
    attnT_h = nc.dram_tensor("attnT_h", [KVD, T], F32).ap()
    cc_in = nc.dram_tensor("cc_in", [2 * KVD, T], F32).ap()
    cc_out = nc.dram_tensor("cc_out", [4 * 2 * KVD, T], F32).ap()
    ar_in = nc.dram_tensor("ar_in", [1, 2], F32).ap()
    ar_out = nc.dram_tensor("ar_out", [1, 2], F32).ap()

    dbg_outs = {}
    if dbg:
        for nm, shp in [("d_row", [1, 64]), ("d_sm", [128, 4]),
                        ("d_xnT", [D, T]), ("d_zT", [R, T]), ("d_Hs", [128, 64]),
                        ("d_Ds", [128, 64]), ("d_scales", [1, 2]), ("d_qT", [KVD, T]),
                        ("d_kv", [4 * 2 * KVD, T]), ("d_attnT", [KVD, T]),
                        ("d_zpm", [R, T]), ("d_wfT", [E, T])]:
            dbg_outs[nm] = nc.dram_tensor(nm, shp, F32, kind="ExternalOutput").ap()

    def sb(name, shape):
        return nc.alloc_sbuf_tensor(name, shape, F32)

    big0 = sb("big0", [128, 16384])
    big1 = sb("big1", [128, 16448])
    qtb = sb("qtb", [128, 4096])
    aring = sb("aring", [128, 2048])
    zT = sb("zT", [R, T])
    zpm = sb("zpm", [R, T])
    ident = sb("ident", [128, 128])
    ones1 = sb("ones1", [1, 128])
    negones1 = sb("negones1", [1, 128])
    ones_col = sb("ones_col", [128, 1])
    b_sb = sb("b_sb", [1, D])
    bm_t = [sb("bm_t0", [R, 512]), sb("bm_t1", [R, 512])]
    limes_sb = sb("limes_sb", [E, R])
    limesh_sb = sb("limesh_sb", [E, R])
    limesT_sb = sb("limesT_sb", [R, E])
    amat_sb = sb("amat_sb", [128, 32 * R])
    w8t_sb = sb("w8t_sb", [128, 32 * E])
    Hs = sb("Hs", [128, 64])
    Ds = sb("Ds", [128, 64])
    stats = sb("stats", [128, 32])
    small = sb("small", [128, 32])
    row = sb("row", [1, 64])
    sq0 = sb("sq0", [128, 256])
    sq1 = sb("sq1", [128, 256])
    negC = sb("negC", [128, 1])
    astage = sb("astage", [128, 128])
    astage2 = sb("astage2", [128, 128])
    gat = sb("gat", [128, 448])
    scal2 = sb("scal2", [1, 2])

    wfT = aring.ap()[0:E, 0:T]

    pk = [nc.alloc_psum_tensor(f"pk{i}", [128, 512], F32) for i in range(8)]

    es = contextlib.ExitStack()
    LD = Sem(es.enter_context(nc.semaphore("LD")))
    ST = Sem(es.enter_context(nc.semaphore("ST")))
    TT = Sem(es.enter_context(nc.semaphore("TT")))
    VV = Sem(es.enter_context(nc.semaphore("VV")))
    SS = Sem(es.enter_context(nc.semaphore("SS")))
    CC = Sem(es.enter_context(nc.semaphore("CC")))
    SC = Sem(es.enter_context(nc.semaphore("SC")))
    EX = Sem(es.enter_context(nc.semaphore("EX")))
    AVK = Sem(es.enter_context(nc.semaphore("AVK")))

    p = Prog()

    def fin():
        p.wait("gpsimd", ST)
        with nc.Block() as block:
            @block.sync
            def _(e):
                for f in p.q["sync"]:
                    f(e)

            @block.tensor
            def _(e):
                for f in p.q["tensor"]:
                    f(e)

            @block.vector
            def _(e):
                for f in p.q["vector"]:
                    f(e)

            @block.scalar
            def _(e):
                for f in p.q["scalar"]:
                    f(e)

            @block.gpsimd
            def _(e):
                for f in p.q["gpsimd"]:
                    f(e)
        es.close()
        return nc, dbg_outs

    p.op("vector", lambda e: e.memset(ones1[:, :], 1.0))
    p.op("vector", lambda e: e.memset(negones1[:, :], -1.0))
    p.op("vector", lambda e: e.memset(ones_col[:, :], 1.0))
    w8_sb = big0[0:8, 0:4096]
    for dst, src in [(ident[:, :], identH), (b_sb[:, :], bvec),
                     (limes_sb[:, :], limes), (limesT_sb[:, :], limesT), (w8_sb, w8)]:
        p.dma("sync", dst, src, LD)
    p.dma("sync", amat_sb.ap().rearrange("p (c r) -> p c r", c=32, r=R),
          amat.rearrange("(c p) r -> p c r", p=128), LD)
    setup_ld = LD.n
    p.wait("vector", LD, setup_ld)
    p.op("vector", lambda e: e.tensor_scalar_mul(limesh_sb[:, :], limes_sb[:, :], 0.5), VV)
    p.wait("tensor", LD, setup_ld)
    w8t_vv = []
    for d in range(32):
        if d >= 2:
            p.wait("tensor", VV, w8t_vv[d - 2])
        p.op("tensor", lambda e, d=d: e.transpose(
            pk[6 + d % 2][0:128, 0:8], w8_sb[0:8, 128 * d:128 * d + 128], ident[0:8, 0:8]), TT)
        p.wait("vector", TT)
        p.op("vector", lambda e, d=d: e.tensor_copy(w8t_sb[:, 8 * d:8 * d + 8], pk[6 + d % 2][0:128, 0:8]), VV)
        w8t_vv.append(VV.n)
    p.wait("tensor", VV)
    w8_done_tt = TT.n

    x3 = x_sl.rearrange("(i h p) d -> i p h d", h=2, p=128)
    fold_vv = []
    fold_st = []
    stats_vv = []
    xn_ss = []
    tr_tt = []
    st_after_tile = []
    for i in range(8):
        xb = big0[:, 8192 * (i % 2):8192 * (i % 2) + 8192]
        if i == 0:
            p.wait("sync", TT, w8_done_tt)
        if i >= 2:
            p.wait("sync", SS, xn_ss[i - 2])
        p.dma("sync", xb.rearrange("p (h d) -> p h d", h=2, d=D), x3[i], LD)
        ld_tile = LD.n
        st = stats
        p.wait("vector", LD, ld_tile)
        p.op("vector", lambda e, xb=xb: e.reduce_sum(
            st[:, 0:2], xb.rearrange("p (h d) -> p h d", h=2, d=D), AX), VV)
        p.wait("scalar", LD, ld_tile)
        if i >= 1:
            p.wait("scalar", VV, stats_vv[i - 1])
        for sub in range(16):
            p.op("scalar", lambda e, xb=xb, sub=sub: e.activation(
                pk[2][:, 0:512], xb[:, 512 * sub:512 * sub + 512],
                ACT.Square, accum_out=st[:, 16 + sub:17 + sub]), SS)
        p.wait("vector", SS)
        p.em("vector", lambda e: e.drain())
        p.vop(lambda e: e.reduce_sum(st[:, 2:3], st[:, 16:24].rearrange("p (a b) -> p a b", a=1), AX))
        p.vop(lambda e: e.reduce_sum(st[:, 3:4], st[:, 24:32].rearrange("p (a b) -> p a b", a=1), AX), VV)
        stats_vv.append(VV.n)
        p.vop(lambda e: e.tensor_scalar_mul(st[:, 0:2], st[:, 0:2], 1.0 / D))
        p.vop(lambda e: e.tensor_scalar_mul(st[:, 2:4], st[:, 2:4], 1.0 / D))
        p.vop(lambda e: e.tensor_tensor(st[:, 4:6], st[:, 0:2], st[:, 0:2], OP.mult))
        p.vop(lambda e: e.tensor_sub(st[:, 4:6], st[:, 2:4], st[:, 4:6]))
        p.vop(lambda e: e.tensor_scalar_add(st[:, 10:12], st[:, 4:6], 1e-5), VV)
        p.wait("scalar", VV)
        p.op("scalar", lambda e: e.activation(st[:, 12:14], st[:, 10:12], ACT.Sqrt), SS)
        p.wait("vector", SS)
        p.vop(lambda e: e.reciprocal(st[:, 6:8], st[:, 12:14]))
        p.vop(lambda e: e.tensor_tensor(st[:, 14:16], st[:, 6:8], st[:, 6:8], OP.mult))
        p.vop(lambda e: e.tensor_tensor(st[:, 14:16], st[:, 14:16], st[:, 10:12], OP.mult))
        p.vop(lambda e: e.tensor_scalar(st[:, 14:16], st[:, 14:16], -0.5, 1.5, OP.mult, OP.add))
        p.vop(lambda e: e.tensor_tensor(st[:, 6:8], st[:, 6:8], st[:, 14:16], OP.mult))
        p.vop(lambda e: e.tensor_tensor(st[:, 8:10], st[:, 0:2], st[:, 6:8], OP.mult))
        p.vop(lambda e: e.tensor_scalar_mul(st[:, 8:10], st[:, 8:10], -1.0), VV)
        p.wait("scalar", VV)
        if i >= 1:
            p.wait("scalar", TT, tr_tt[i - 1])
        for h in range(2):
            p.op("scalar", lambda e, xb=xb, h=h: e.activation(
                big1[:, 4096 * h:4096 * h + 4096], xb[:, 4096 * h:4096 * h + 4096], ACT.Identity,
                bias=st[:, 8 + h:9 + h], scale=st[:, 6 + h:7 + h]), SS)
        xn_ss.append(SS.n)
        p.wait("tensor", SS, xn_ss[-1])
        if i >= 1:
            p.wait("tensor", ST, st_after_tile[i - 1])
        evac_vv = []
        for c in range(64):
            bank = 4 + (c % 4)
            if c >= 4:
                p.wait("tensor", VV, evac_vv[c - 4])
            p.op("tensor", lambda e, c=c, bank=bank: e.transpose(
                pk[bank][:, 0:128], big1[:, 128 * c:128 * c + 128], ident[:, :]), TT)
            p.wait("vector", TT)
            h_, d_ = c // 32, c % 32
            p.op("vector", lambda e, h_=h_, d_=d_, bank=bank: e.tensor_copy(
                big1[:, 8192 + 256 * d_ + 128 * h_:8192 + 256 * d_ + 128 * h_ + 128],
                pk[bank][:, 0:128]), VV)
            evac_vv.append(VV.n)
        tr_tt.append(TT.n)
        xnT_tile = big1[:, 8192:16384]
        p.wait("tensor", VV, evac_vv[-1])
        for d in range(32):
            p.op("tensor", lambda e, d=d: e.matmul(
                pk[0][0:R, 0:256], amat_sb[:, 32 * d:32 * d + 32],
                xnT_tile[:, 256 * d:256 * d + 256], start=(d == 0), stop=(d == 31)),
                TT if d == 31 else None)
        for d in range(32):
            p.op("tensor", lambda e, d=d: e.matmul(
                pk[1][0:128, 0:8], xnT_tile[:, 256 * d + 1:256 * d + 256:2],
                w8t_sb[:, 8 * d:8 * d + 8], start=(d == 0), stop=(d == 31)),
                TT if d == 31 else None)
        p.wait("vector", TT)
        p.op("vector", lambda e, i=i: e.tensor_copy(zT[:, 256 * i:256 * i + 256], pk[0][0:R, 0:256]), VV)
        p.op("vector", lambda e, i=i: e.tensor_copy(Hs[:, 8 * i:8 * i + 8], pk[1][0:128, 0:8]), VV)
        p.wait("gpsimd", VV)
        for gq in range(4):
            src = xnT_tile[:, 2048 * gq:2048 * gq + 2048].rearrange("p (c j) -> p c j", c=8, j=256)
            dst = xnT_h.rearrange("(c p) t -> p c t", p=128)[:, 8 * gq:8 * gq + 8, 256 * i:256 * i + 256]
            p.dma("gpsimd", dst, src, ST)
        st_after_tile.append(ST.n)
        p.wait("tensor", VV)
        for fc in range(8 * i, 8 * i + 8):
            d, hf = fc // 2, fc % 2
            ftmp = qtb[:, 2048 * (fc % 2):2048 * (fc % 2) + 2048]
            fout = aring[:, 1024 * (fc % 2):1024 * (fc % 2) + 512]
            if fc >= 2:
                p.wait("sync", VV, fold_vv[fc - 2])
            p.dma("sync", ftmp, wq[128 * d:128 * d + 128, 2048 * hf:2048 * hf + 2048], LD)
            p.wait("vector", LD)
            if fc >= 2:
                p.wait("vector", ST, fold_st[fc - 2])
            o4 = fout.rearrange("p (kv j) -> p kv j", kv=4, j=128)
            v8 = ftmp.rearrange("p (kv g j) -> p kv g j", kv=4, g=4, j=128)
            p.op("vector", lambda e, v8=v8, o4=o4: e.tensor_tensor(o4, v8[:, :, 0, :], v8[:, :, 1, :], OP.add))
            p.op("vector", lambda e, v8=v8, o4=o4: e.tensor_tensor(o4, o4, v8[:, :, 2, :], OP.add))
            p.op("vector", lambda e, v8=v8, o4=o4: e.tensor_tensor(o4, o4, v8[:, :, 3, :], OP.add), VV)
            fold_vv.append(VV.n)
            p.wait("gpsimd", VV)
            p.dma("gpsimd", wqe_h[128 * d:128 * d + 128, 512 * hf:512 * hf + 512], fout, ST)
            fold_st.append(ST.n)

    for c in range(8):
        p.op("tensor", lambda e, c=c: e.matmul(
            pk[2][0:128, 0:8], zT[:, 256 * c + 1:256 * c + 256:2], limesT_sb[:, :],
            start=True, stop=True), TT)
        p.wait("vector", TT)
        p.op("vector", lambda e, c=c: e.tensor_copy(Ds[:, 8 * c:8 * c + 8], pk[2][0:128, 0:8]), VV)
        p.wait("tensor", VV)
    for j, buf in ((0, Hs), (1, Ds)):
        p.vop(lambda e, buf=buf: e.reduce_max(
            small[:, 0:1], buf[:, :], AX, apply_absolute_value=True), VV)
        p.wait("tensor", VV)
        p.op("tensor", lambda e: e.transpose(pk[3][0:1, 0:128], small[:, 0:1], ident[:, :]), TT)
        p.wait("vector", TT)
        p.vop(lambda e, j=j: e.reduce_max(row[0:1, j:j + 1], pk[3][0:1, 0:128], AX))
        p.vop(lambda e, j=j: e.tensor_scalar_max(row[0:1, j:j + 1], row[0:1, j:j + 1], 1e-6), VV)
        p.wait("tensor", VV)
    p.wait("gpsimd", VV)
    p.dma("gpsimd", ar_in, row[0:1, 0:2], ST)

    if dbg:
        p.wait("gpsimd", VV)
        p.dma("gpsimd", dbg_outs["d_zT"], zT[:, :], ST)
        p.dma("gpsimd", dbg_outs["d_Hs"], Hs[:, :], ST)
        p.dma("gpsimd", dbg_outs["d_Ds"], Ds[:, :], ST)
        p.dma("gpsimd", dbg_outs["d_xnT"], xnT_h, ST)
        p.wait("gpsimd", ST)
        p.wait("gpsimd", TT, TT.n)
        p.wait("gpsimd", VV, VV.n)
        p.wait("gpsimd", SS, SS.n)
        p.op("gpsimd", lambda e: e.collective_compute(
            "AllReduce", OP.max, replica_groups=[[0, 1, 2, 3, 4, 5, 6, 7]],
            ins=[ar_in.opt()], outs=[ar_out.opt()]), CC)
        p.wait("gpsimd", CC)
        p.dma("gpsimd", dbg_outs["d_scales"], ar_out, ST)
        p.dma("gpsimd", dbg_outs["d_row"], row[0:1, 0:64], ST)
        p.dma("gpsimd", dbg_outs["d_sm"], small[:, 0:4], ST)
    if stage <= 1:
        return fin()

    xnT_src = xnT_h.rearrange("(c p) t -> p c t", p=128)
    slice_tt = []
    evac_hist = []
    stg_st = []
    quarter_tt = []

    p.wait("sync", SS, xn_ss[-1])
    p.wait("sync", ST, st_after_tile[-1])
    p.wait("sync", TT, tr_tt[-1])
    p.wait("vector", ST, ST.n)

    def emit_proj(weight_ap, dst_ap):
        for quarter in range(4):
            src = weight_ap[:, 256 * quarter:256 * quarter + 256].rearrange("(c p) o -> p c o", p=128)
            woff = 8192 * (quarter % 2)
            dstw = big1[:, woff:woff + 8192].rearrange("p (c o) -> p c o", c=32, o=256)
            if len(quarter_tt) >= 2:
                p.wait("sync", TT, quarter_tt[-2])
            for gq in range(2):
                p.dma("sync", dstw[:, 16 * gq:16 * gq + 16, :], src[:, 16 * gq:16 * gq + 16, :], LD)
            wld = LD.n
            for tsl in range(8):
                xb = big0[:, 8192 * (tsl % 2):8192 * (tsl % 2) + 8192]
                if len(slice_tt) >= 2:
                    p.wait("sync", TT, slice_tt[-2])
                xdst = xb.rearrange("p (c j) -> p c j", c=32, j=256)
                for gq in range(4):
                    p.dma("sync", xdst[:, 8 * gq:8 * gq + 8, :],
                          xnT_src[:, 8 * gq:8 * gq + 8, 256 * tsl:256 * tsl + 256], LD)
                sld = LD.n
                p.wait("tensor", LD, max(sld, wld))
                for oc in range(2):
                    bank = len(evac_hist) % 2
                    if len(evac_hist) >= 2:
                        p.wait("tensor", VV, evac_hist[-2])
                    for d in range(32):
                        p.op("tensor", lambda e, d=d, oc=oc, bank=bank, xb=xb, woff=woff: e.matmul(
                            pk[bank][:, 0:256], big1[:, woff + 256 * d + 128 * oc:woff + 256 * d + 128 * oc + 128],
                            xb[:, 256 * d:256 * d + 256], start=(d == 0), stop=(d == 31)),
                            TT if d == 31 else None)
                    p.wait("vector", TT)
                    stg = aring[:, 512 * (len(stg_st) % 2):512 * (len(stg_st) % 2) + 256]
                    if len(stg_st) >= 2:
                        p.wait("vector", ST, stg_st[-2])
                    p.op("vector", lambda e, stg=stg, bank=bank: e.tensor_copy(stg, pk[bank][:, 0:256]), VV)
                    evac_hist.append(VV.n)
                    p.wait("gpsimd", VV)
                    orow = 256 * quarter + 128 * oc
                    p.dma("gpsimd", dst_ap[orow:orow + 128, 256 * tsl:256 * tsl + 256], stg, ST)
                    stg_st.append(ST.n)
                slice_tt.append(TT.n)
            quarter_tt.append(TT.n)

    emit_proj(wk, cc_in[0:KVD, :])
    emit_proj(wv, cc_in[KVD:2 * KVD, :])
    emit_proj(wqe_h, qT_h)

    p.wait("gpsimd", ST)
    p.wait("gpsimd", TT, TT.n)
    p.wait("gpsimd", VV, VV.n)
    p.wait("gpsimd", SS, SS.n)
    p.wait("gpsimd", LD, LD.n)
    p.op("gpsimd", lambda e: e.collective_compute(
        "AllReduce", OP.max, replica_groups=[[0, 1, 2, 3, 4, 5, 6, 7]],
        ins=[ar_in.opt()], outs=[ar_out.opt()]), CC)
    p.wait("gpsimd", CC)
    p.op("gpsimd", lambda e: e.collective_compute(
        "AllGather", OP.bypass, replica_groups=[[0, 1, 2, 3], [4, 5, 6, 7]],
        ins=[cc_in.opt()], outs=[cc_out.opt()]), CC)

    if dbg:
        p.wait("gpsimd", ST)
        p.wait("gpsimd", CC)
        p.dma("gpsimd", dbg_outs["d_qT"], qT_h, ST)
        p.dma("gpsimd", dbg_outs["d_kv"], cc_out, ST)
    if stage <= 2:
        return fin()

    p.wait("sync", CC, CC.n)
    p.wait("sync", TT, slice_tt[-1])
    vones = big1[:, 8192:16448]
    vtraw = big1[:, 0:8192]
    p.wait("vector", TT, slice_tt[-1])
    p.op("vector", lambda e: e.memset(
        vones.rearrange("p (c j) -> p c j", c=64, j=129)[:, :, 128:129], 1.0), VV)
    vones_init_vv = VV.n

    kt_consumed = [0] * NH
    vt_consumed = [0] * NH
    av_done = [0] * NH
    norm_consumed = [0] * NH

    for h in range(NH):
        ktb = big0[:, 8192 * (h % 2):8192 * (h % 2) + 8192]
        qb = qtb[:, 2048 * (h % 2):2048 * (h % 2) + 2048]
        if h >= 2:
            p.wait("sync", TT, kt_consumed[h - 2])
            p.wait("sync", SS, norm_consumed[h - 2])
        for r_ in range(4):
            p.dma("sync", ktb[:, 2048 * r_:2048 * r_ + 2048],
                  cc_out[2048 * r_ + HD * h:2048 * r_ + HD * h + HD, :], LD)
        if h >= 1:
            p.wait("sync", TT, vt_consumed[h - 1])
        for r_ in range(4):
            p.dma("sync", vtraw[:, 2048 * r_:2048 * r_ + 2048],
                  cc_out[2048 * r_ + KVD + HD * h:2048 * r_ + KVD + HD * h + HD, :], LD)
        p.dma("sync", qb, qT_h[HD * h:HD * h + HD, :], LD)
        ld_head = LD.n

        p.wait("tensor", LD, ld_head)
        if h >= 1:
            p.wait("tensor", AVK, av_done[h - 1])
        p.wait("tensor", VV)
        evac_vv = []
        for c in range(64):
            bank = 6 + (c % 2)
            if c >= 2:
                p.wait("tensor", VV, evac_vv[c - 2])
            p.op("tensor", lambda e, c=c, bank=bank: e.transpose(
                pk[bank][:, 0:128], vtraw[:, 128 * c:128 * c + 128], ident[:, :]), TT)
            p.wait("vector", TT)
            p.op("vector", lambda e, c=c, bank=bank: e.tensor_copy(
                vones[:, 129 * c:129 * c + 128], pk[bank][:, 0:128]), VV)
            evac_vv.append(VV.n)
        vt_consumed[h] = TT.n
        p.wait("tensor", VV, evac_vv[-1])

        p.vop(lambda e: e.memset(row[0:1, 2:4], 0.0), VV)
        p.wait("scalar", LD, ld_head)
        p.wait("scalar", VV)
        mm_tt = []
        for which, buf, ntile, slot in (("k", ktb, 32, 2), ("q", qb, 8, 3)):
            for t in range(ntile):
                sqb = sq0 if t % 2 == 0 else sq1
                if len(mm_tt) >= 2:
                    p.wait("scalar", TT, mm_tt[-2])
                p.op("scalar", lambda e, buf=buf, t=t, sqb=sqb: e.activation(
                    sqb[:, :], buf[:, 256 * t:256 * t + 256], ACT.Square), SS)
                p.wait("tensor", SS)
                p.wait("tensor", VV)
                p.op("tensor", lambda e, sqb=sqb: e.matmul(
                    pk[5][0:1, 0:256], ones_col[:, :], sqb[:, :], start=True, stop=True), TT)
                mm_tt.append(TT.n)
                p.wait("vector", TT)
                p.vop(lambda e: e.reduce_max(row[0:1, 4:5], pk[5][0:1, 0:256], AX))
                p.vop(lambda e, slot=slot: e.tensor_tensor(
                    row[0:1, slot:slot + 1], row[0:1, slot:slot + 1], row[0:1, 4:5], OP.max), VV)
        norm_consumed[h] = SS.n
        p.vop(lambda e: e.tensor_tensor(row[0:1, 5:6], row[0:1, 2:3], row[0:1, 3:4], OP.mult), VV)
        p.wait("scalar", VV)
        p.op("scalar", lambda e: e.activation(row[0:1, 6:7], row[0:1, 5:6], ACT.Sqrt), SS)
        p.wait("vector", SS)
        p.vop(lambda e: e.tensor_scalar_mul(row[0:1, 6:7], row[0:1, 6:7], 1.01 * INV_SQRT_HD), VV)
        p.wait("tensor", VV)
        p.op("tensor", lambda e: e.matmul(pk[5][0:128, 0:1], negones1[:, :], row[0:1, 6:7],
                                          start=True, stop=True), TT)
        p.wait("vector", TT)
        p.wait("vector", EX, EX.n)
        p.op("vector", lambda e: e.tensor_copy(negC[:, :], pk[5][0:128, 0:1]), VV)
        negc_vv = VV.n
        p.wait("scalar", VV, negc_vv)
        p.wait("tensor", VV, negc_vv)

        for qt in range(4):
            sc_base = SC.n
            ex_base = EX.n
            avk_base = AVK.n
            epi_vv = VV.n
            for step in range(65):
                if step < 64:
                    kc = step
                    sbank = 4 + (kc % 3)
                    if kc >= 3:
                        p.wait("tensor", EX, ex_base + kc - 2)
                    p.op("tensor", lambda e, kc=kc, qt=qt, sbank=sbank, ktb=ktb, qb=qb: e.matmul(
                        pk[sbank][:, 0:512], ktb[:, 128 * kc:128 * kc + 128],
                        qb[:, 512 * qt:512 * qt + 512], start=True, stop=True), SC)
                    p.wait("scalar", SC, sc_base + kc + 1)
                    if kc >= 4:
                        p.wait("scalar", AVK, avk_base + kc - 3)
                    p.op("scalar", lambda e, kc=kc, sbank=sbank: e.activation(
                        aring[:, 512 * (kc % 4):512 * (kc % 4) + 512], pk[sbank][:, 0:512],
                        ACT.Exp, bias=negC[:, 0:1], scale=INV_SQRT_HD), EX)
                if step >= 1:
                    j = step - 1
                    p.wait("tensor", EX, ex_base + j + 1)
                    for qs in range(4):
                        p.op("tensor", lambda e, j=j, qs=qs: e.matmul(
                            pk[qs][:, 0:129],
                            aring[:, 512 * (j % 4) + 128 * qs:512 * (j % 4) + 128 * qs + 128],
                            vones[:, 129 * j:129 * j + 129], start=(j == 0), stop=(j == 63)),
                            AVK if qs == 3 else None)
            p.wait("vector", AVK, AVK.n)
            for qs in range(4):
                p.vop(lambda e, qs=qs: e.tensor_copy(small[:, 8:9], pk[qs][:, 128:129]))
                p.vop(lambda e: e.reciprocal(small[:, 9:10], small[:, 8:9]))
                p.op("vector", lambda e, qs=qs: e.tensor_scalar(
                    astage[:, :], pk[qs][:, 0:128], small[0:128, 9:10], None, OP.mult), VV)
                p.wait("tensor", VV)
                p.op("tensor", lambda e: e.transpose(pk[7][:, 0:128], astage[:, :], ident[:, :]), TT)
                p.wait("vector", TT)
                p.wait("vector", ST)
                p.op("vector", lambda e: e.tensor_copy(astage2[:, :], pk[7][:, 0:128]), VV)
                p.wait("gpsimd", VV)
                p.dma("gpsimd", attnT_h[HD * h:HD * h + HD,
                                        512 * qt + 128 * qs:512 * qt + 128 * qs + 128], astage2[:, :], ST)
            p.wait("tensor", VV)
        kt_consumed[h] = TT.n
        av_done[h] = AVK.n

    if dbg:
        p.wait("gpsimd", ST)
        p.dma("gpsimd", dbg_outs["d_attnT"], attnT_h, ST)
    if stage <= 3:
        return fin()

    p.wait("gpsimd", CC, CC.n)
    p.dma("gpsimd", scal2[0:1, 0:2], ar_out, LD)
    gld = LD.n
    p.wait("vector", LD, gld)
    p.vop(lambda e: e.reciprocal(row[0:1, 8:10], scal2[0:1, 0:2]))
    p.vop(lambda e: e.tensor_scalar_mul(row[0:1, 8:10], row[0:1, 8:10], 0.5), VV)
    for j in range(2):
        p.wait("tensor", VV)
        p.op("tensor", lambda e, j=j: e.matmul(pk[7][0:128, 0:1], ones1[:, :], row[0:1, 8 + j:9 + j],
                                               start=True, stop=True), TT)
        p.wait("vector", TT)
        p.vop(lambda e, j=j: e.tensor_copy(small[:, 10 + j:11 + j], pk[7][0:128, 0:1]), VV)
        p.wait("tensor", VV)
    L = gat[:, 0:64]
    P_ = gat[:, 64:128]
    p.vop(lambda e: e.tensor_scalar(L, Hs[:, :], small[0:128, 10:11], None, OP.mult))
    p.vop(lambda e: e.scalar_tensor_tensor(L, Ds[:, :], small[0:128, 11:12], L, OP.mult, OP.add), VV)
    p.wait("scalar", VV)
    p.op("scalar", lambda e: e.activation(P_, L, ACT.Exp), SS)
    p.wait("vector", SS)
    g3 = P_.rearrange("p (c e) -> p c e", c=8, e=E)
    p.vop(lambda e: e.reduce_sum(small[:, 12:20], g3, AX))
    p.vop(lambda e: e.reciprocal(small[:, 20:28], small[:, 12:20]))
    for c in range(8):
        p.vop(lambda e, c=c: e.tensor_scalar(
            P_[:, 8 * c:8 * c + 8], P_[:, 8 * c:8 * c + 8], small[0:128, 20 + c:21 + c], None, OP.mult))
    for c in range(8):
        p.vop(lambda e, c=c: e.max(out=gat[:, 192 + 8 * c:192 + 8 * c + 8],
                                   in_=P_[:, 8 * c:8 * c + 8]))
    for c in range(8):
        p.vop(lambda e, c=c: e.tensor_scalar(
            gat[:, 256 + 8 * c:256 + 8 * c + 8], P_[:, 8 * c:8 * c + 8],
            gat[0:128, 192 + 8 * c + 1:192 + 8 * c + 2], None, OP.subtract), VV)
    p.wait("scalar", VV)
    p.op("scalar", lambda e: e.activation(gat[:, 320:384], gat[:, 256:320], ACT.Sigmoid, scale=2.0), SS)
    p.wait("vector", SS)
    W_ = gat[:, 384:448]
    p.vop(lambda e: e.tensor_tensor(W_, P_, gat[:, 320:384], OP.mult))
    w3 = W_.rearrange("p (c e) -> p c e", c=8, e=E)
    p.vop(lambda e: e.reduce_sum(small[:, 12:20], w3, AX))
    p.vop(lambda e: e.tensor_scalar_add(small[:, 12:20], small[:, 12:20], 1e-9))
    p.vop(lambda e: e.reciprocal(small[:, 20:28], small[:, 12:20]))
    for c in range(8):
        p.vop(lambda e, c=c: e.tensor_scalar(
            W_[:, 8 * c:8 * c + 8], W_[:, 8 * c:8 * c + 8], small[0:128, 20 + c:21 + c], None, OP.mult), VV)
    for c in range(8):
        p.wait("tensor", VV)
        p.op("tensor", lambda e, c=c: e.transpose(pk[7][0:8, 0:128], W_[:, 8 * c:8 * c + 8], ident[:, :]), TT)
        p.wait("vector", TT)
        p.op("vector", lambda e, c=c: e.tensor_copy(wfT[0:8, 256 * c:256 * c + 256:2], pk[7][0:8, 0:128]))
        p.op("vector", lambda e, c=c: e.tensor_copy(wfT[0:8, 256 * c + 1:256 * c + 256:2], pk[7][0:8, 0:128]), VV)
    p.wait("tensor", VV)
    for c in range(4):
        p.op("tensor", lambda e, c=c: e.matmul(
            pk[3][0:R, 0:512], limesh_sb[:, :], wfT[0:8, 512 * c:512 * c + 512],
            start=True, stop=True), TT)
        p.wait("vector", TT)
        p.op("vector", lambda e, c=c: e.tensor_tensor(
            zpm[:, 512 * c:512 * c + 512], pk[3][0:R, 0:512], zT[:, 512 * c:512 * c + 512], OP.mult), VV)
        p.wait("tensor", VV)

    if dbg:
        p.wait("gpsimd", VV)
        p.dma("gpsimd", dbg_outs["d_zpm"], zpm[:, :], ST)
        p.dma("gpsimd", dbg_outs["d_wfT"], wfT[0:E, 0:T], ST)

    wo_src = wo.rearrange("(c p) o -> p c o", p=128)
    wT_src = wT.rearrange("(c p) o -> p c o", p=128)
    attnT_src = attnT_h.rearrange("(c p) t -> p c t", p=128)

    p.wait("sync", TT, TT.n)
    p.wait("sync", AVK, AVK.n)
    p.wait("sync", VV, VV.n)
    p.wait("tensor", VV, VV.n)

    blk_tt = []
    hd_tt = []
    smm_tt = []
    wt_hist = []
    silu_ss = []
    sg_vv = []
    xst_vv = []

    for tb in range(4):
        if tb >= 1:
            p.wait("sync", TT, hd_tt[-1])
        for gq in range(4):
            p.dma("sync", big0.ap().rearrange("p (c j) -> p c j", c=32, j=512)[:, 8 * gq:8 * gq + 8, :],
                  xnT_src[:, 8 * gq:8 * gq + 8, 512 * tb:512 * tb + 512], LD)
        if tb >= 1:
            p.wait("scalar", TT, smm_tt[-1])
        p.dma("scalar", qtb.ap().rearrange("p (c j) -> p c j", c=8, j=512),
              attnT_src[:, :, 512 * tb:512 * tb + 512], LD)
        blk_ld = LD.n
        p.wait("tensor", LD, blk_ld)
        for o in range(8):
            wop = big1[:, 4096 * (o % 2):4096 * (o % 2) + 4096]
            if len(smm_tt) >= 2:
                p.wait("scalar", TT, smm_tt[-2])
            p.dma("scalar", wop.rearrange("p (c j) -> p c j", c=8, j=512),
                  wo_src[:, :, 512 * o:512 * o + 512], LD)
            wo_ld = LD.n
            if len(hd_tt) >= 2:
                p.wait("scalar", TT, hd_tt[-2])
            p.dma("scalar", bm_t[o % 2][:, :], bmat[0:R, 512 * o:512 * o + 512], LD)
            bm_ld = LD.n
            for d in range(32):
                wtp = big1[:, 8192 + 512 * (d % 2):8192 + 512 * (d % 2) + 512]
                if len(wt_hist) >= 2:
                    p.wait("sync", TT, wt_hist[-2])
                p.dma("sync", wtp, wT_src[:, d, 512 * o:512 * o + 512], LD)
                p.wait("tensor", LD, LD.n)
                if d == 0 and silu_ss:
                    p.wait("tensor", SS, silu_ss[-1])
                for ts in range(4):
                    p.op("tensor", lambda e, d=d, ts=ts, wtp=wtp: e.matmul(
                        pk[ts][:, 0:512], big0[:, 512 * d + 128 * ts:512 * d + 128 * ts + 128],
                        wtp, start=(d == 0), stop=False), TT if ts == 3 else None)
                wt_hist.append(TT.n)
            for ts in range(4):
                p.op("tensor", lambda e, ts=ts, o=o: e.matmul(
                    pk[ts][:, 0:512], ones1[:, :], b_sb[0:1, 512 * o:512 * o + 512],
                    start=False, stop=False), None)
                if ts == 0:
                    p.wait("tensor", LD, bm_ld)
                p.op("tensor", lambda e, o=o, ts=ts, tb=tb: e.matmul(
                    pk[ts][:, 0:512], zpm[:, 512 * tb + 128 * ts:512 * tb + 128 * ts + 128],
                    bm_t[o % 2][:, :], start=False, stop=True),
                    TT if ts == 3 else None)
            hdone_tt = TT.n
            hd_tt.append(hdone_tt)
            p.wait("scalar", TT, hdone_tt)
            if sg_vv:
                p.wait("scalar", VV, sg_vv[-1])
            for ts in range(4):
                p.op("scalar", lambda e, ts=ts: e.activation(
                    big1[:, 9216 + 512 * ts:9216 + 512 * ts + 512], pk[ts][:, 0:512], ACT.Silu), SS)
            silu_ss.append(SS.n)
            p.wait("tensor", LD, wo_ld)
            if sg_vv:
                p.wait("tensor", VV, sg_vv[-1])
            for dv in range(8):
                for ts in range(4):
                    p.op("tensor", lambda e, dv=dv, ts=ts, wop=wop: e.matmul(
                        pk[4 + ts][:, 0:512], qtb[:, 512 * dv + 128 * ts:512 * dv + 128 * ts + 128],
                        wop[:, 512 * dv:512 * dv + 512], start=(dv == 0), stop=(dv == 7)),
                        TT if (dv == 7 and ts == 3) else None)
            smm_tt.append(TT.n)
            for ts in range(4):
                xst = aring[:, 512 * (ts % 2):512 * (ts % 2) + 512]
                if len(xst_vv) >= 2:
                    p.wait("scalar", VV, xst_vv[-2])
                p.dma("scalar", xst, x_sl[512 * tb + 128 * ts:512 * tb + 128 * ts + 128,
                                          512 * o:512 * o + 512], LD)
                xl = LD.n
                if ts == 0:
                    p.wait("vector", TT, smm_tt[-1])
                    p.wait("vector", SS, silu_ss[-1])
                p.wait("vector", LD, xl)
                ost = aring[:, 1024 + 512 * (ts % 2):1024 + 512 * (ts % 2) + 512]
                if ts >= 2:
                    p.wait("vector", ST)
                p.op("vector", lambda e, ts=ts, ost=ost: e.tensor_tensor(
                    ost, pk[4 + ts][:, 0:512], big1[:, 9216 + 512 * ts:9216 + 512 * ts + 512], OP.mult))
                p.op("vector", lambda e, ts=ts, ost=ost, xst=xst: e.tensor_tensor(ost, ost, xst, OP.add), VV)
                xst_vv.append(VV.n)
                p.wait("gpsimd", VV)
                p.dma("gpsimd", out_sl[512 * tb + 128 * ts:512 * tb + 128 * ts + 128,
                                       512 * o:512 * o + 512], ost, ST)
            sg_vv.append(VV.n)
        blk_tt.append(TT.n)

    return fin()


def make_inputs(inputs):
    x = np.ascontiguousarray(np.asarray(inputs["L0_attn_x"], dtype=np.float32))
    W = np.asarray(inputs["L0_ffn_W"], dtype=np.float32)
    common = {
        "wq": np.ascontiguousarray(np.asarray(inputs["L0_attn_wq"], dtype=np.float32)),
        "wk": np.ascontiguousarray(np.asarray(inputs["L0_attn_wk"], dtype=np.float32)),
        "wv": np.ascontiguousarray(np.asarray(inputs["L0_attn_wv"], dtype=np.float32)),
        "wo": np.ascontiguousarray(np.asarray(inputs["L0_attn_wo"], dtype=np.float32)),
        "wT": np.ascontiguousarray(W.T),
        "w8": np.ascontiguousarray(W[0:E, :]),
        "bvec": np.ascontiguousarray(np.asarray(inputs["L0_ffn_b"], dtype=np.float32)).reshape(1, D),
        "amat": np.ascontiguousarray(np.asarray(inputs["L0_ffn_A"], dtype=np.float32)),
        "bmat": np.ascontiguousarray(np.asarray(inputs["L0_ffn_Bmat"], dtype=np.float32)),
        "limes": np.ascontiguousarray(np.asarray(inputs["L0_ffn_LiMEs"], dtype=np.float32)),
        "limesT": np.ascontiguousarray(np.asarray(inputs["L0_ffn_LiMEs"], dtype=np.float32).T),
        "identH": np.eye(128, dtype=np.float32),
    }
    in_maps = []
    for c in range(8):
        b, t0 = c // 4, (c % 4) * T
        m = dict(common)
        m["x_sl"] = np.ascontiguousarray(x[b, t0:t0 + T, :])
        in_maps.append(m)
    return in_maps


def kernel(**inputs):
    in_maps = make_inputs(inputs)
    nc, _ = build(dbg=False, stage=4)
    res = run_bass_kernel_spmd(nc, in_maps, core_ids=list(range(8)))
    out = np.empty((2, S, D), dtype=np.float32)
    for c in range(8):
        b, t0 = c // 4, (c % 4) * T
        out[b, t0:t0 + T, :] = np.asarray(res.results[c]["out_sl"])
    return out


# version 04
# speedup vs baseline: 1.1526x; 1.0171x over previous
import contextlib
import math
import numpy as np
import concourse.bass as bass
import concourse.mybir as mybir
from concourse.bass_utils import run_bass_kernel_spmd

F32 = mybir.dt.float32
BF16 = mybir.dt.bfloat16
AX = mybir.AxisListType.X
OP = mybir.AluOpType
ACT = mybir.ActivationFunctionType

T = 2048
D = 4096
KVD = 1024
NH = 8
HD = 128
S = 8192
E = 8
R = 32
INV_SQRT_HD = 1.0 / math.sqrt(HD)

ENGS = ("sync", "tensor", "vector", "scalar", "gpsimd")


class Sem:
    def __init__(self, handle):
        self.h = handle
        self.n = 0


class Prog:
    def __init__(self):
        self.q = {e: [] for e in ENGS}

    def em(self, eng, f):
        self.q[eng].append(f)

    def wait(self, eng, sem, val=None):
        v = sem.n if val is None else val
        if v <= 0:
            return
        self.em(eng, lambda e, s=sem.h, v=v: e.wait_ge(s, v))

    def dma(self, eng, out, in_, sem):
        sem.n += 16
        self.em(eng, lambda e, o=out, i=in_, s=sem.h: e.dma_start(out=o, in_=i).then_inc(s, 16))

    def op(self, eng, fn, sem=None, inc=1):
        if sem is None:
            self.em(eng, fn)
        else:
            sem.n += inc
            self.em(eng, lambda e, f=fn, s=sem.h, k=inc: f(e).then_inc(s, k))

    def vop(self, fn, sem=None):
        self.op("vector", fn, sem)
        self.em("vector", lambda e: e.drain())


def build(dbg: bool = False, stage: int = 4):
    nc = bass.Bass("TRN2", target_bir_lowering=False)

    x_sl = nc.dram_tensor("x_sl", [T, D], F32, kind="ExternalInput").ap()
    wq = nc.dram_tensor("wq", [D, D], F32, kind="ExternalInput").ap()
    wk = nc.dram_tensor("wk", [D, KVD], F32, kind="ExternalInput").ap()
    wv = nc.dram_tensor("wv", [D, KVD], F32, kind="ExternalInput").ap()
    wo = nc.dram_tensor("wo", [KVD, D], F32, kind="ExternalInput").ap()
    wT = nc.dram_tensor("wT", [D, D], F32, kind="ExternalInput").ap()
    w8 = nc.dram_tensor("w8", [E, D], F32, kind="ExternalInput").ap()
    bvec = nc.dram_tensor("bvec", [1, D], F32, kind="ExternalInput").ap()
    amat = nc.dram_tensor("amat", [D, R], F32, kind="ExternalInput").ap()
    bmat = nc.dram_tensor("bmat", [R, D], F32, kind="ExternalInput").ap()
    limes = nc.dram_tensor("limes", [E, R], F32, kind="ExternalInput").ap()
    limesT = nc.dram_tensor("limesT", [R, E], F32, kind="ExternalInput").ap()
    identH = nc.dram_tensor("identH", [128, 128], F32, kind="ExternalInput").ap()
    out_sl = nc.dram_tensor("out_sl", [T, D], F32, kind="ExternalOutput").ap()

    xnThi_h = nc.dram_tensor("xnThi_h", [D, T], BF16).ap()
    xnTlo_h = nc.dram_tensor("xnTlo_h", [D, T], BF16).ap()
    wqe_h = nc.dram_tensor("wqe_h", [D, KVD], F32).ap()
    qThi_h = nc.dram_tensor("qThi_h", [KVD, T], BF16).ap()
    qTlo_h = nc.dram_tensor("qTlo_h", [KVD, T], BF16).ap()
    attnThi_h = nc.dram_tensor("attnThi_h", [KVD, T], BF16).ap()
    attnTlo_h = nc.dram_tensor("attnTlo_h", [KVD, T], BF16).ap()
    cc_in = nc.dram_tensor("cc_in", [2 * KVD, T], F32).ap()
    cc_out = nc.dram_tensor("cc_out", [4 * 2 * KVD, T], F32).ap()
    ar_in = nc.dram_tensor("ar_in", [1, 2], F32).ap()
    ar_out = nc.dram_tensor("ar_out", [1, 2], F32).ap()

    dbg_outs = {}
    if dbg:
        for nm, shp in [("d_row", [1, 64]), ("d_sm", [128, 4]),
                        ("d_zT", [R, T]), ("d_Hs", [128, 64]),
                        ("d_Ds", [128, 64]), ("d_scales", [1, 2]),
                        ("d_kv", [4 * 2 * KVD, T]),
                        ("d_zpm", [R, T]), ("d_wfT", [E, T])]:
            dbg_outs[nm] = nc.dram_tensor(nm, shp, F32, kind="ExternalOutput").ap()
        dbg_outs["d_attnT"] = nc.dram_tensor("d_attnT", [KVD, T], BF16, kind="ExternalOutput").ap()
        dbg_outs["d_xnT"] = nc.dram_tensor("d_xnT", [D, T], BF16, kind="ExternalOutput").ap()
        dbg_outs["d_qT"] = nc.dram_tensor("d_qT", [KVD, T], BF16, kind="ExternalOutput").ap()

    def sb(name, shape):
        return nc.alloc_sbuf_tensor(name, shape, F32)

    big0 = sb("big0", [128, 16384])
    big1 = sb("big1", [128, 16448])
    qtb = sb("qtb", [128, 4096])
    aring = sb("aring", [128, 2048])
    zT = sb("zT", [R, T])
    zpm = sb("zpm", [R, T])
    ident = sb("ident", [128, 128])
    ones1 = sb("ones1", [1, 128])
    negones1 = sb("negones1", [1, 128])
    ones_col = sb("ones_col", [128, 1])
    b_sb = sb("b_sb", [1, D])
    bm_t = [sb("bm_t0", [R, 512]), sb("bm_t1", [R, 512])]
    limes_sb = sb("limes_sb", [E, R])
    limesh_sb = sb("limesh_sb", [E, R])
    limesT_sb = sb("limesT_sb", [R, E])
    amat_sb = sb("amat_sb", [128, 32 * R])
    w8t_sb = sb("w8t_sb", [128, 32 * E])
    Hs = sb("Hs", [128, 64])
    Ds = sb("Ds", [128, 64])
    stats = sb("stats", [128, 32])
    small = sb("small", [128, 32])
    row = sb("row", [1, 64])
    sq0 = sb("sq0", [128, 256])
    sq1 = sb("sq1", [128, 256])
    negC = sb("negC", [128, 1])
    astage = sb("astage", [128, 128])
    astage2 = sb("astage2", [128, 128])
    gat = sb("gat", [128, 448])
    scal2 = sb("scal2", [1, 2])

    wfT = aring.ap()[0:E, 0:T]
    big0_16 = big0.bitcast(BF16)
    big1_16 = big1.bitcast(BF16)
    qtb_16 = qtb.bitcast(BF16)
    aring_16 = aring.bitcast(BF16)
    etmp = sb("etmp", [128, 1024])

    pk = [nc.alloc_psum_tensor(f"pk{i}", [128, 512], F32) for i in range(8)]

    es = contextlib.ExitStack()
    LD = Sem(es.enter_context(nc.semaphore("LD")))
    ST = Sem(es.enter_context(nc.semaphore("ST")))
    TT = Sem(es.enter_context(nc.semaphore("TT")))
    VV = Sem(es.enter_context(nc.semaphore("VV")))
    SS = Sem(es.enter_context(nc.semaphore("SS")))
    CC = Sem(es.enter_context(nc.semaphore("CC")))
    SC = Sem(es.enter_context(nc.semaphore("SC")))
    EX = Sem(es.enter_context(nc.semaphore("EX")))
    AVK = Sem(es.enter_context(nc.semaphore("AVK")))
    AEX = Sem(es.enter_context(nc.semaphore("AEX")))
    SLD = Sem(es.enter_context(nc.semaphore("SLD")))
    GLD = Sem(es.enter_context(nc.semaphore("GLD")))
    XL = Sem(es.enter_context(nc.semaphore("XL")))
    FQ = Sem(es.enter_context(nc.semaphore("FQ")))
    WST = Sem(es.enter_context(nc.semaphore("WST")))
    XSL = Sem(es.enter_context(nc.semaphore("XSL")))
    KTL = Sem(es.enter_context(nc.semaphore("KTL")))
    VTL = Sem(es.enter_context(nc.semaphore("VTL")))
    QL = Sem(es.enter_context(nc.semaphore("QL")))
    XNB = Sem(es.enter_context(nc.semaphore("XNB")))
    ATB = Sem(es.enter_context(nc.semaphore("ATB")))
    WOS = Sem(es.enter_context(nc.semaphore("WOS")))
    BMS = Sem(es.enter_context(nc.semaphore("BMS")))
    WTS = Sem(es.enter_context(nc.semaphore("WTS")))
    XLS = Sem(es.enter_context(nc.semaphore("XLS")))

    p = Prog()

    def fin():
        p.wait("gpsimd", ST)
        with nc.Block() as block:
            @block.sync
            def _(e):
                for f in p.q["sync"]:
                    f(e)

            @block.tensor
            def _(e):
                for f in p.q["tensor"]:
                    f(e)

            @block.vector
            def _(e):
                for f in p.q["vector"]:
                    f(e)

            @block.scalar
            def _(e):
                for f in p.q["scalar"]:
                    f(e)

            @block.gpsimd
            def _(e):
                for f in p.q["gpsimd"]:
                    f(e)
        es.close()
        return nc, dbg_outs

    p.op("vector", lambda e: e.memset(ones1[:, :], 1.0))
    p.op("vector", lambda e: e.memset(negones1[:, :], -1.0))
    p.op("vector", lambda e: e.memset(ones_col[:, :], 1.0))
    w8_sb = big0[0:8, 0:4096]
    for dst, src in [(ident[:, :], identH), (b_sb[:, :], bvec),
                     (limes_sb[:, :], limes), (limesT_sb[:, :], limesT), (w8_sb, w8)]:
        p.dma("sync", dst, src, LD)
    p.dma("sync", amat_sb.ap().rearrange("p (c r) -> p c r", c=32, r=R),
          amat.rearrange("(c p) r -> p c r", p=128), LD)
    setup_ld = LD.n
    p.wait("vector", LD, setup_ld)
    p.op("vector", lambda e: e.tensor_scalar_mul(limesh_sb[:, :], limes_sb[:, :], 0.5), VV)
    p.wait("tensor", LD, setup_ld)
    w8t_vv = []
    for d in range(32):
        if d >= 2:
            p.wait("tensor", VV, w8t_vv[d - 2])
        p.op("tensor", lambda e, d=d: e.transpose(
            pk[6 + d % 2][0:128, 0:8], w8_sb[0:8, 128 * d:128 * d + 128], ident[0:8, 0:8]), TT)
        p.wait("vector", TT)
        p.op("vector", lambda e, d=d: e.tensor_copy(w8t_sb[:, 8 * d:8 * d + 8], pk[6 + d % 2][0:128, 0:8]), VV)
        w8t_vv.append(VV.n)
    p.wait("tensor", VV)
    w8_done_tt = TT.n

    x3 = x_sl.rearrange("(i h p) d -> i p h d", h=2, p=128)
    fold_vv = []
    fold_st = []
    stats_vv = []
    xn_ss = []
    tr_tt = []
    st_after_tile = []
    for i in range(8):
        xb = big0[:, 8192 * (i % 2):8192 * (i % 2) + 8192]
        if i == 0:
            p.wait("sync", TT, w8_done_tt)
        if i >= 2:
            p.wait("sync", SS, xn_ss[i - 2])
            p.wait("sync", ST, st_after_tile[i - 2])
        p.dma("sync", xb.rearrange("p (h d) -> p h d", h=2, d=D), x3[i], XL)
        ld_tile = XL.n
        st = stats
        p.wait("vector", XL, ld_tile)
        p.op("vector", lambda e, xb=xb: e.reduce_sum(
            st[:, 0:2], xb.rearrange("p (h d) -> p h d", h=2, d=D), AX), VV)
        p.wait("scalar", XL, ld_tile)
        if i >= 1:
            p.wait("scalar", VV, stats_vv[i - 1])
        for sub in range(16):
            p.op("scalar", lambda e, xb=xb, sub=sub: e.activation(
                pk[2][:, 0:512], xb[:, 512 * sub:512 * sub + 512],
                ACT.Square, accum_out=st[:, 16 + sub:17 + sub]), SS)
        p.wait("vector", SS)
        p.em("vector", lambda e: e.drain())
        p.vop(lambda e: e.reduce_sum(st[:, 2:3], st[:, 16:24].rearrange("p (a b) -> p a b", a=1), AX))
        p.vop(lambda e: e.reduce_sum(st[:, 3:4], st[:, 24:32].rearrange("p (a b) -> p a b", a=1), AX), VV)
        stats_vv.append(VV.n)
        p.vop(lambda e: e.tensor_scalar_mul(st[:, 0:2], st[:, 0:2], 1.0 / D))
        p.vop(lambda e: e.tensor_scalar_mul(st[:, 2:4], st[:, 2:4], 1.0 / D))
        p.vop(lambda e: e.tensor_tensor(st[:, 4:6], st[:, 0:2], st[:, 0:2], OP.mult))
        p.vop(lambda e: e.tensor_sub(st[:, 4:6], st[:, 2:4], st[:, 4:6]))
        p.vop(lambda e: e.tensor_scalar_add(st[:, 10:12], st[:, 4:6], 1e-5), VV)
        p.wait("scalar", VV)
        p.op("scalar", lambda e: e.activation(st[:, 12:14], st[:, 10:12], ACT.Sqrt), SS)
        p.wait("vector", SS)
        p.vop(lambda e: e.reciprocal(st[:, 6:8], st[:, 12:14]))
        p.vop(lambda e: e.tensor_tensor(st[:, 14:16], st[:, 6:8], st[:, 6:8], OP.mult))
        p.vop(lambda e: e.tensor_tensor(st[:, 14:16], st[:, 14:16], st[:, 10:12], OP.mult))
        p.vop(lambda e: e.tensor_scalar(st[:, 14:16], st[:, 14:16], -0.5, 1.5, OP.mult, OP.add))
        p.vop(lambda e: e.tensor_tensor(st[:, 6:8], st[:, 6:8], st[:, 14:16], OP.mult))
        p.vop(lambda e: e.tensor_tensor(st[:, 8:10], st[:, 0:2], st[:, 6:8], OP.mult))
        p.vop(lambda e: e.tensor_scalar_mul(st[:, 8:10], st[:, 8:10], -1.0), VV)
        p.wait("scalar", VV)
        if i >= 1:
            p.wait("scalar", TT, tr_tt[i - 1])
        for h in range(2):
            p.op("scalar", lambda e, xb=xb, h=h: e.activation(
                big1[:, 4096 * h:4096 * h + 4096], xb[:, 4096 * h:4096 * h + 4096], ACT.Identity,
                bias=st[:, 8 + h:9 + h], scale=st[:, 6 + h:7 + h]), SS)
        xn_ss.append(SS.n)
        p.wait("tensor", SS, xn_ss[-1])
        if i >= 1:
            p.wait("tensor", ST, st_after_tile[i - 1])
        evac_vv = []
        for c in range(64):
            bank = 4 + (c % 4)
            if c >= 4:
                p.wait("tensor", VV, evac_vv[c - 4])
            p.op("tensor", lambda e, c=c, bank=bank: e.transpose(
                pk[bank][:, 0:128], big1[:, 128 * c:128 * c + 128], ident[:, :]), TT)
            p.wait("vector", TT)
            h_, d_ = c // 32, c % 32
            p.op("vector", lambda e, h_=h_, d_=d_, bank=bank: e.tensor_copy(
                big1[:, 8192 + 256 * d_ + 128 * h_:8192 + 256 * d_ + 128 * h_ + 128],
                pk[bank][:, 0:128]), VV)
            evac_vv.append(VV.n)
        tr_tt.append(TT.n)
        xnT_tile = big1[:, 8192:16384]
        p.wait("tensor", VV, evac_vv[-1])
        for d in range(32):
            p.op("tensor", lambda e, d=d: e.matmul(
                pk[0][0:R, 0:256], amat_sb[:, 32 * d:32 * d + 32],
                xnT_tile[:, 256 * d:256 * d + 256], start=(d == 0), stop=(d == 31)),
                TT if d == 31 else None)
        for d in range(32):
            p.op("tensor", lambda e, d=d: e.matmul(
                pk[1][0:128, 0:8], xnT_tile[:, 256 * d + 1:256 * d + 256:2],
                w8t_sb[:, 8 * d:8 * d + 8], start=(d == 0), stop=(d == 31)),
                TT if d == 31 else None)
        p.wait("vector", TT)
        p.op("vector", lambda e, i=i: e.tensor_copy(zT[:, 256 * i:256 * i + 256], pk[0][0:R, 0:256]), VV)
        p.op("vector", lambda e, i=i: e.tensor_copy(Hs[:, 8 * i:8 * i + 8], pk[1][0:128, 0:8]), VV)
        p.wait("vector", SS, xn_ss[i])
        hilo = big0_16[:, 16384 * (i % 2):16384 * (i % 2) + 16384]
        hi_t = hilo[:, 0:8192]
        lo_t = hilo[:, 8192:16384]
        p.op("vector", lambda e, hi_t=hi_t: e.tensor_copy(hi_t, xnT_tile))
        p.op("vector", lambda e, hi_t=hi_t, lo_t=lo_t: e.tensor_tensor(lo_t, xnT_tile, hi_t, OP.subtract), VV)
        p.wait("gpsimd", VV)
        for comp, dst_h in ((hi_t, xnThi_h), (lo_t, xnTlo_h)):
            for gq in range(2):
                src = comp[:, 4096 * gq:4096 * gq + 4096].rearrange("p (c j) -> p c j", c=16, j=256)
                dst = dst_h.rearrange("(c p) t -> p c t", p=128)[:, 16 * gq:16 * gq + 16, 256 * i:256 * i + 256]
                p.dma("gpsimd", dst, src, ST)
        st_after_tile.append(ST.n)
        p.wait("tensor", VV)
        for fc in range(8 * i, 8 * i + 8):
            d, hf = fc // 2, fc % 2
            ftmp = qtb[:, 2048 * (fc % 2):2048 * (fc % 2) + 2048]
            fout = aring[:, 1024 * (fc % 2):1024 * (fc % 2) + 512]
            if fc >= 2:
                p.wait("sync", VV, fold_vv[fc - 2])
            p.dma("sync", ftmp, wq[128 * d:128 * d + 128, 2048 * hf:2048 * hf + 2048], FQ)
            p.wait("vector", FQ)
            if fc >= 2:
                p.wait("vector", ST, fold_st[fc - 2])
            o4 = fout.rearrange("p (kv j) -> p kv j", kv=4, j=128)
            v8 = ftmp.rearrange("p (kv g j) -> p kv g j", kv=4, g=4, j=128)
            p.op("vector", lambda e, v8=v8, o4=o4: e.tensor_tensor(o4, v8[:, :, 0, :], v8[:, :, 1, :], OP.add))
            p.op("vector", lambda e, v8=v8, o4=o4: e.tensor_tensor(o4, o4, v8[:, :, 2, :], OP.add))
            p.op("vector", lambda e, v8=v8, o4=o4: e.tensor_tensor(o4, o4, v8[:, :, 3, :], OP.add), VV)
            fold_vv.append(VV.n)
            p.wait("gpsimd", VV)
            p.dma("gpsimd", wqe_h[128 * d:128 * d + 128, 512 * hf:512 * hf + 512], fout, ST)
            fold_st.append(ST.n)

    for c in range(8):
        p.op("tensor", lambda e, c=c: e.matmul(
            pk[2][0:128, 0:8], zT[:, 256 * c + 1:256 * c + 256:2], limesT_sb[:, :],
            start=True, stop=True), TT)
        p.wait("vector", TT)
        p.op("vector", lambda e, c=c: e.tensor_copy(Ds[:, 8 * c:8 * c + 8], pk[2][0:128, 0:8]), VV)
        p.wait("tensor", VV)
    for j, buf in ((0, Hs), (1, Ds)):
        p.vop(lambda e, buf=buf: e.reduce_max(
            small[:, 0:1], buf[:, :], AX, apply_absolute_value=True), VV)
        p.wait("tensor", VV)
        p.op("tensor", lambda e: e.transpose(pk[3][0:1, 0:128], small[:, 0:1], ident[:, :]), TT)
        p.wait("vector", TT)
        p.vop(lambda e, j=j: e.reduce_max(row[0:1, j:j + 1], pk[3][0:1, 0:128], AX))
        p.vop(lambda e, j=j: e.tensor_scalar_max(row[0:1, j:j + 1], row[0:1, j:j + 1], 1e-6), VV)
        p.wait("tensor", VV)
    p.wait("gpsimd", VV)
    p.dma("gpsimd", ar_in, row[0:1, 0:2], ST)

    if dbg:
        p.wait("gpsimd", VV)
        p.dma("gpsimd", dbg_outs["d_zT"], zT[:, :], ST)
        p.dma("gpsimd", dbg_outs["d_Hs"], Hs[:, :], ST)
        p.dma("gpsimd", dbg_outs["d_Ds"], Ds[:, :], ST)
        p.dma("gpsimd", dbg_outs["d_xnT"], xnThi_h, ST)
        p.wait("gpsimd", ST)
        p.wait("gpsimd", TT, TT.n)
        p.wait("gpsimd", VV, VV.n)
        p.wait("gpsimd", SS, SS.n)
        p.op("gpsimd", lambda e: e.collective_compute(
            "AllReduce", OP.max, replica_groups=[[0, 1, 2, 3, 4, 5, 6, 7]],
            ins=[ar_in.opt()], outs=[ar_out.opt()]), CC)
        p.wait("gpsimd", CC)
        p.dma("gpsimd", dbg_outs["d_scales"], ar_out, ST)
        p.dma("gpsimd", dbg_outs["d_row"], row[0:1, 0:64], ST)
        p.dma("gpsimd", dbg_outs["d_sm"], small[:, 0:4], ST)
    if stage <= 1:
        return fin()

    slice_tt = []
    evac_hist = []
    stg_st = []
    quarter_tt = []
    wsplit_vv = []

    p.wait("sync", SS, xn_ss[-1])
    p.wait("sync", ST, st_after_tile[-1])
    p.wait("sync", TT, tr_tt[-1])
    p.wait("vector", ST, ST.n)

    def emit_proj(weight_ap, dst_ap, q_out=False):
        for eighth in range(8):
            src = weight_ap[:, 128 * eighth:128 * eighth + 128].rearrange("(c p) o -> p c o", p=128)
            stg_w = big1[:, 0:4096].rearrange("p (c o) -> p c o", c=32, o=128)
            if wsplit_vv:
                p.wait("sync", VV, wsplit_vv[-1])
            p.dma("sync", stg_w, src, WST)
            wld = WST.n
            wslot = big1_16[:, 8192 + 8192 * (eighth % 2):8192 + 8192 * (eighth % 2) + 8192]
            w_hi = wslot[:, 0:4096]
            w_lo = wslot[:, 4096:8192]
            p.wait("vector", WST, wld)
            if len(quarter_tt) >= 2:
                p.wait("vector", TT, quarter_tt[-2])
            p.op("vector", lambda e, w_hi=w_hi: e.tensor_copy(w_hi, big1[:, 0:4096]))
            p.op("vector", lambda e, w_hi=w_hi, w_lo=w_lo: e.tensor_tensor(
                w_lo, big1[:, 0:4096], w_hi, OP.subtract), VV)
            wsplit_vv.append(VV.n)
            p.wait("tensor", VV, wsplit_vv[-1])
            for tsl in range(8):
                xs = big0_16[:, 16384 * (tsl % 2):16384 * (tsl % 2) + 16384]
                x_hi = xs[:, 0:8192].rearrange("p (c j) -> p c j", c=32, j=256)
                x_lo = xs[:, 8192:16384].rearrange("p (c j) -> p c j", c=32, j=256)
                if len(slice_tt) >= 2:
                    p.wait("sync", TT, slice_tt[-2])
                for comp, src_h in ((x_hi, xnThi_h), (x_lo, xnTlo_h)):
                    for gq in range(2):
                        p.dma("sync", comp[:, 16 * gq:16 * gq + 16, :],
                              src_h.rearrange("(c p) t -> p c t", p=128)[:, 16 * gq:16 * gq + 16,
                                                                         256 * tsl:256 * tsl + 256], XSL)
                sld = XSL.n
                p.wait("tensor", XSL, sld)
                bank = len(evac_hist) % 2
                if len(evac_hist) >= 2:
                    p.wait("tensor", VV, evac_hist[-2])
                xs_flat = big0_16[:, 16384 * (tsl % 2):16384 * (tsl % 2) + 16384]
                for d in range(32):
                    for pi, (wsl, xoff) in enumerate(((0, 0), (0, 8192), (4096, 0))):
                        p.op("tensor", lambda e, d=d, bank=bank, wsl=wsl, xoff=xoff, xs_flat=xs_flat, wslot=wslot, pi=pi: e.matmul(
                            pk[bank][:, 0:256], wslot[:, wsl + 128 * d:wsl + 128 * d + 128],
                            xs_flat[:, xoff + 256 * d:xoff + 256 * d + 256],
                            start=(d == 0 and pi == 0), stop=(d == 31 and pi == 2)),
                            TT if (d == 31 and pi == 2) else None)
                p.wait("vector", TT)
                if not q_out:
                    stg = aring[:, 512 * (len(stg_st) % 2):512 * (len(stg_st) % 2) + 256]
                    if len(stg_st) >= 2:
                        p.wait("vector", ST, stg_st[-2])
                    p.op("vector", lambda e, stg=stg, bank=bank: e.tensor_copy(stg, pk[bank][:, 0:256]), VV)
                    evac_hist.append(VV.n)
                    p.wait("gpsimd", VV)
                    p.dma("gpsimd", dst_ap[128 * eighth:128 * eighth + 128,
                                           256 * tsl:256 * tsl + 256], stg, ST)
                    stg_st.append(ST.n)
                else:
                    sslot = aring_16[:, 1024 * (len(stg_st) % 2):1024 * (len(stg_st) % 2) + 1024]
                    s_hi = sslot[:, 0:256]
                    s_lo = sslot[:, 256:512]
                    if len(stg_st) >= 2:
                        p.wait("vector", ST, stg_st[-2])
                    p.op("vector", lambda e, s_hi=s_hi, bank=bank: e.tensor_copy(s_hi, pk[bank][:, 0:256]))
                    p.op("vector", lambda e, s_hi=s_hi, s_lo=s_lo, bank=bank: e.tensor_tensor(
                        s_lo, pk[bank][:, 0:256], s_hi, OP.subtract), VV)
                    evac_hist.append(VV.n)
                    p.wait("gpsimd", VV)
                    p.dma("gpsimd", qThi_h[128 * eighth:128 * eighth + 128,
                                           256 * tsl:256 * tsl + 256], s_hi, ST)
                    p.dma("gpsimd", qTlo_h[128 * eighth:128 * eighth + 128,
                                           256 * tsl:256 * tsl + 256], s_lo, ST)
                    stg_st.append(ST.n)
                slice_tt.append(TT.n)
            quarter_tt.append(TT.n)

    emit_proj(wk, cc_in[0:KVD, :])
    emit_proj(wv, cc_in[KVD:2 * KVD, :])
    emit_proj(wqe_h, None, q_out=True)

    p.wait("gpsimd", ST)
    p.wait("gpsimd", TT, TT.n)
    p.wait("gpsimd", VV, VV.n)
    p.wait("gpsimd", SS, SS.n)
    p.wait("gpsimd", LD, LD.n)
    p.op("gpsimd", lambda e: e.collective_compute(
        "AllReduce", OP.max, replica_groups=[[0, 1, 2, 3, 4, 5, 6, 7]],
        ins=[ar_in.opt()], outs=[ar_out.opt()]), CC)
    p.wait("gpsimd", CC)
    p.op("gpsimd", lambda e: e.collective_compute(
        "AllGather", OP.bypass, replica_groups=[[0, 1, 2, 3], [4, 5, 6, 7]],
        ins=[cc_in.opt()], outs=[cc_out.opt()]), CC)

    if dbg:
        p.wait("gpsimd", ST)
        p.wait("gpsimd", CC)
        p.dma("gpsimd", dbg_outs["d_qT"], qThi_h, ST)
        p.dma("gpsimd", dbg_outs["d_kv"], cc_out, ST)
    if stage <= 2:
        return fin()

    p.wait("sync", CC, CC.n)
    p.wait("sync", TT, slice_tt[-1])
    vtraw = big1[:, 0:8192]
    vones_hi = big1_16[:, 16384:24640]
    vones_lo = big1_16[:, 24640:32896]
    ktstg = big0[:, 8192:16384]
    kt_hi = big0_16[:, 0:8192]
    kt_lo = big0_16[:, 8192:16384]
    p.wait("vector", TT, slice_tt[-1])
    p.op("vector", lambda e: e.memset(
        vones_hi.rearrange("p (c j) -> p c j", c=64, j=129)[:, :, 128:129], 1.0))
    p.op("vector", lambda e: e.memset(
        vones_lo.rearrange("p (c j) -> p c j", c=64, j=129)[:, :, 128:129], 0.0), VV)
    vones_init_vv = VV.n

    kt_consumed = [0] * NH
    vt_consumed = [0] * NH
    av_done = [0] * NH
    norm_consumed = [0] * NH

    ksplit_vv = [0] * NH
    for h in range(NH):
        qslot = qtb_16[:, 4096 * (h % 2):4096 * (h % 2) + 4096]
        q_hi = qslot[:, 0:2048]
        q_lo = qslot[:, 2048:4096]
        if h >= 1:
            p.wait("sync", VV, ksplit_vv[h - 1])
        for r_ in range(4):
            p.dma("sync", ktstg[:, 2048 * r_:2048 * r_ + 2048],
                  cc_out[2048 * r_ + HD * h:2048 * r_ + HD * h + HD, :], KTL)
        ktld = KTL.n
        if h >= 1:
            p.wait("sync", TT, vt_consumed[h - 1])
        for r_ in range(4):
            p.dma("sync", vtraw[:, 2048 * r_:2048 * r_ + 2048],
                  cc_out[2048 * r_ + KVD + HD * h:2048 * r_ + KVD + HD * h + HD, :], VTL)
        if h >= 2:
            p.wait("sync", TT, kt_consumed[h - 2])
            p.wait("sync", SS, norm_consumed[h - 2])
        p.dma("sync", q_hi, qThi_h[HD * h:HD * h + HD, :], QL)
        p.dma("sync", q_lo, qTlo_h[HD * h:HD * h + HD, :], QL)
        qld = QL.n
        vtld = VTL.n
        p.wait("vector", KTL, ktld)
        if h >= 1:
            p.wait("vector", TT, kt_consumed[h - 1])
            p.wait("vector", SS, norm_consumed[h - 1])
        p.op("vector", lambda e: e.tensor_copy(kt_hi, ktstg))
        p.op("vector", lambda e: e.tensor_tensor(kt_lo, ktstg, kt_hi, OP.subtract), VV)
        ksplit_vv[h] = VV.n

        p.wait("tensor", VTL, vtld)
        p.wait("tensor", QL, qld)
        if h >= 1:
            p.wait("tensor", AVK, av_done[h - 1])
        p.wait("tensor", VV)
        evac_vv = []
        for c in range(64):
            bank = 6 + (c % 2)
            if c >= 2:
                p.wait("tensor", VV, evac_vv[c - 2])
            p.op("tensor", lambda e, c=c, bank=bank: e.transpose(
                pk[bank][:, 0:128], vtraw[:, 128 * c:128 * c + 128], ident[:, :]), TT)
            p.wait("vector", TT)
            p.op("vector", lambda e, c=c, bank=bank: e.tensor_copy(
                vones_hi[:, 129 * c:129 * c + 128], pk[bank][:, 0:128]))
            p.op("vector", lambda e, c=c, bank=bank: e.tensor_tensor(
                vones_lo[:, 129 * c:129 * c + 128], pk[bank][:, 0:128],
                vones_hi[:, 129 * c:129 * c + 128], OP.subtract), VV)
            evac_vv.append(VV.n)
        vt_consumed[h] = TT.n
        p.wait("tensor", VV, evac_vv[-1])

        p.vop(lambda e: e.memset(row[0:1, 2:4], 0.0), VV)
        p.wait("scalar", QL, qld)
        p.wait("scalar", VV)
        mm_tt = []
        for which, buf, ntile, slot in (("k", kt_hi, 32, 2), ("q", q_hi, 8, 3)):
            for t in range(ntile):
                sqb = sq0 if t % 2 == 0 else sq1
                if len(mm_tt) >= 2:
                    p.wait("scalar", TT, mm_tt[-2])
                p.op("scalar", lambda e, buf=buf, t=t, sqb=sqb: e.activation(
                    sqb[:, :], buf[:, 256 * t:256 * t + 256], ACT.Square), SS)
                p.wait("tensor", SS)
                p.wait("tensor", VV)
                p.op("tensor", lambda e, sqb=sqb: e.matmul(
                    pk[5][0:1, 0:256], ones_col[:, :], sqb[:, :], start=True, stop=True), TT)
                mm_tt.append(TT.n)
                p.wait("vector", TT)
                p.vop(lambda e: e.reduce_max(row[0:1, 4:5], pk[5][0:1, 0:256], AX))
                p.vop(lambda e, slot=slot: e.tensor_tensor(
                    row[0:1, slot:slot + 1], row[0:1, slot:slot + 1], row[0:1, 4:5], OP.max), VV)
        norm_consumed[h] = SS.n
        p.vop(lambda e: e.tensor_tensor(row[0:1, 5:6], row[0:1, 2:3], row[0:1, 3:4], OP.mult), VV)
        p.wait("scalar", VV)
        p.op("scalar", lambda e: e.activation(row[0:1, 6:7], row[0:1, 5:6], ACT.Sqrt), SS)
        p.wait("vector", SS)
        p.vop(lambda e: e.tensor_scalar_mul(row[0:1, 6:7], row[0:1, 6:7], 1.02 * INV_SQRT_HD), VV)
        p.wait("tensor", VV)
        p.op("tensor", lambda e: e.matmul(pk[5][0:128, 0:1], negones1[:, :], row[0:1, 6:7],
                                          start=True, stop=True), TT)
        p.wait("vector", TT)
        p.wait("vector", EX, EX.n)
        p.op("vector", lambda e: e.tensor_copy(negC[:, :], pk[5][0:128, 0:1]), VV)
        negc_vv = VV.n
        p.wait("scalar", VV, negc_vv)
        p.wait("tensor", VV, negc_vv)

        for qt in range(4):
            sc_base = SC.n
            ex_base = EX.n
            avk_base = AVK.n
            aex_base = AEX.n
            for step in range(65):
                if step < 64:
                    kc = step
                    sbank = 4 + (kc % 3)
                    if kc >= 3:
                        p.wait("tensor", EX, ex_base + kc - 2)
                    for pi, (ka, qa) in enumerate(((kt_hi, q_hi), (kt_hi, q_lo), (kt_lo, q_hi))):
                        p.op("tensor", lambda e, kc=kc, qt=qt, sbank=sbank, ka=ka, qa=qa, pi=pi: e.matmul(
                            pk[sbank][:, 0:512], ka[:, 128 * kc:128 * kc + 128],
                            qa[:, 512 * qt:512 * qt + 512], start=(pi == 0), stop=(pi == 2)),
                            SC if pi == 2 else None)
                    p.wait("scalar", SC, sc_base + kc + 1)
                    if kc >= 2:
                        p.wait("scalar", AEX, aex_base + kc - 1)
                    p.op("scalar", lambda e, kc=kc, sbank=sbank: e.activation(
                        etmp[:, 512 * (kc % 2):512 * (kc % 2) + 512], pk[sbank][:, 0:512],
                        ACT.Exp, bias=negC[:, 0:1], scale=INV_SQRT_HD), EX)
                    p.wait("vector", EX, ex_base + kc + 1)
                    if kc >= 4:
                        p.wait("vector", AVK, avk_base + kc - 3)
                    asl = aring_16[:, 1024 * (kc % 4):1024 * (kc % 4) + 1024]
                    p.op("vector", lambda e, kc=kc, asl=asl: e.tensor_copy(
                        asl[:, 0:512], etmp[:, 512 * (kc % 2):512 * (kc % 2) + 512]))
                    p.op("vector", lambda e, kc=kc, asl=asl: e.tensor_tensor(
                        asl[:, 512:1024], etmp[:, 512 * (kc % 2):512 * (kc % 2) + 512],
                        asl[:, 0:512], OP.subtract), AEX)
                if step >= 1:
                    j = step - 1
                    p.wait("tensor", AEX, aex_base + j + 1)
                    ajs = aring_16[:, 1024 * (j % 4):1024 * (j % 4) + 1024]
                    for qs in range(4):
                        for pi, (aoff, vsrc) in enumerate(((0, vones_hi), (0, vones_lo), (512, vones_hi))):
                            p.op("tensor", lambda e, j=j, qs=qs, ajs=ajs, aoff=aoff, vsrc=vsrc, pi=pi: e.matmul(
                                pk[qs][:, 0:129],
                                ajs[:, aoff + 128 * qs:aoff + 128 * qs + 128],
                                vsrc[:, 129 * j:129 * j + 129],
                                start=(j == 0 and pi == 0), stop=(j == 63 and pi == 2)),
                                AVK if (qs == 3 and pi == 2) else None)
            p.wait("vector", AVK, AVK.n)
            for qs in range(4):
                p.vop(lambda e, qs=qs: e.tensor_copy(small[:, 8:9], pk[qs][:, 128:129]))
                p.vop(lambda e: e.reciprocal(small[:, 9:10], small[:, 8:9]))
                p.op("vector", lambda e, qs=qs: e.tensor_scalar(
                    astage[:, :], pk[qs][:, 0:128], small[0:128, 9:10], None, OP.mult), VV)
                p.wait("tensor", VV)
                p.op("tensor", lambda e: e.transpose(pk[7][:, 0:128], astage[:, :], ident[:, :]), TT)
                p.wait("vector", TT)
                p.wait("vector", ST)
                ast16 = astage2.bitcast(BF16)
                p.op("vector", lambda e, ast16=ast16: e.tensor_copy(ast16[:, 0:128], pk[7][:, 0:128]))
                p.op("vector", lambda e, ast16=ast16: e.tensor_tensor(
                    ast16[:, 128:256], pk[7][:, 0:128], ast16[:, 0:128], OP.subtract), VV)
                p.wait("gpsimd", VV)
                p.dma("gpsimd", attnThi_h[HD * h:HD * h + HD,
                                          512 * qt + 128 * qs:512 * qt + 128 * qs + 128], ast16[:, 0:128], ST)
                p.dma("gpsimd", attnTlo_h[HD * h:HD * h + HD,
                                          512 * qt + 128 * qs:512 * qt + 128 * qs + 128], ast16[:, 128:256], ST)
            p.wait("tensor", VV)
        kt_consumed[h] = TT.n
        av_done[h] = AVK.n

    if dbg:
        p.wait("gpsimd", ST)
        p.dma("gpsimd", dbg_outs["d_attnT"], attnThi_h, ST)
    if stage <= 3:
        return fin()

    p.wait("gpsimd", CC, CC.n)
    p.dma("gpsimd", scal2[0:1, 0:2], ar_out, GLD)
    p.wait("vector", GLD, GLD.n)
    p.vop(lambda e: e.reciprocal(row[0:1, 8:10], scal2[0:1, 0:2]))
    p.vop(lambda e: e.tensor_scalar_mul(row[0:1, 8:10], row[0:1, 8:10], 0.5), VV)
    for j in range(2):
        p.wait("tensor", VV)
        p.op("tensor", lambda e, j=j: e.matmul(pk[7][0:128, 0:1], ones1[:, :], row[0:1, 8 + j:9 + j],
                                               start=True, stop=True), TT)
        p.wait("vector", TT)
        p.vop(lambda e, j=j: e.tensor_copy(small[:, 10 + j:11 + j], pk[7][0:128, 0:1]), VV)
        p.wait("tensor", VV)
    L = gat[:, 0:64]
    P_ = gat[:, 64:128]
    p.vop(lambda e: e.tensor_scalar(L, Hs[:, :], small[0:128, 10:11], None, OP.mult))
    p.vop(lambda e: e.scalar_tensor_tensor(L, Ds[:, :], small[0:128, 11:12], L, OP.mult, OP.add), VV)
    p.wait("scalar", VV)
    p.op("scalar", lambda e: e.activation(P_, L, ACT.Exp), SS)
    p.wait("vector", SS)
    g3 = P_.rearrange("p (c e) -> p c e", c=8, e=E)
    p.vop(lambda e: e.reduce_sum(small[:, 12:20], g3, AX))
    p.vop(lambda e: e.reciprocal(small[:, 20:28], small[:, 12:20]))
    for c in range(8):
        p.vop(lambda e, c=c: e.tensor_scalar(
            P_[:, 8 * c:8 * c + 8], P_[:, 8 * c:8 * c + 8], small[0:128, 20 + c:21 + c], None, OP.mult))
    for c in range(8):
        p.vop(lambda e, c=c: e.max(out=gat[:, 192 + 8 * c:192 + 8 * c + 8],
                                   in_=P_[:, 8 * c:8 * c + 8]))
    for c in range(8):
        p.vop(lambda e, c=c: e.tensor_scalar(
            gat[:, 256 + 8 * c:256 + 8 * c + 8], P_[:, 8 * c:8 * c + 8],
            gat[0:128, 192 + 8 * c + 1:192 + 8 * c + 2], None, OP.subtract), VV)
    p.wait("scalar", VV)
    p.op("scalar", lambda e: e.activation(gat[:, 320:384], gat[:, 256:320], ACT.Sigmoid, scale=2.0), SS)
    p.wait("vector", SS)
    W_ = gat[:, 384:448]
    p.vop(lambda e: e.tensor_tensor(W_, P_, gat[:, 320:384], OP.mult))
    w3 = W_.rearrange("p (c e) -> p c e", c=8, e=E)
    p.vop(lambda e: e.reduce_sum(small[:, 12:20], w3, AX))
    p.vop(lambda e: e.tensor_scalar_add(small[:, 12:20], small[:, 12:20], 1e-9))
    p.vop(lambda e: e.reciprocal(small[:, 20:28], small[:, 12:20]))
    for c in range(8):
        p.vop(lambda e, c=c: e.tensor_scalar(
            W_[:, 8 * c:8 * c + 8], W_[:, 8 * c:8 * c + 8], small[0:128, 20 + c:21 + c], None, OP.mult), VV)
    for c in range(8):
        p.wait("tensor", VV)
        p.op("tensor", lambda e, c=c: e.transpose(pk[7][0:8, 0:128], W_[:, 8 * c:8 * c + 8], ident[:, :]), TT)
        p.wait("vector", TT)
        p.op("vector", lambda e, c=c: e.tensor_copy(wfT[0:8, 256 * c:256 * c + 256:2], pk[7][0:8, 0:128]))
        p.op("vector", lambda e, c=c: e.tensor_copy(wfT[0:8, 256 * c + 1:256 * c + 256:2], pk[7][0:8, 0:128]), VV)
    p.wait("tensor", VV)
    for c in range(4):
        p.op("tensor", lambda e, c=c: e.matmul(
            pk[3][0:R, 0:512], limesh_sb[:, :], wfT[0:8, 512 * c:512 * c + 512],
            start=True, stop=True), TT)
        p.wait("vector", TT)
        p.op("vector", lambda e, c=c: e.tensor_tensor(
            zpm[:, 512 * c:512 * c + 512], pk[3][0:R, 0:512], zT[:, 512 * c:512 * c + 512], OP.mult), VV)
        p.wait("tensor", VV)

    if dbg:
        p.wait("gpsimd", VV)
        p.dma("gpsimd", dbg_outs["d_zpm"], zpm[:, :], ST)
        p.dma("gpsimd", dbg_outs["d_wfT"], wfT[0:E, 0:T], ST)

    wo_src = wo.rearrange("(c p) o -> p c o", p=128)
    wT_src = wT.rearrange("(c p) o -> p c o", p=128)
    at_hi_src = attnThi_h.rearrange("(c p) t -> p c t", p=128)
    at_lo_src = attnTlo_h.rearrange("(c p) t -> p c t", p=128)
    xn_hi_src = xnThi_h.rearrange("(c p) t -> p c t", p=128)
    xn_lo_src = xnTlo_h.rearrange("(c p) t -> p c t", p=128)
    woslot = [big1_16[:, 14336 + 8192 * si:14336 + 8192 * si + 8192] for si in range(2)]
    wtslot = [big1_16[:, 30720 + 1024 * si:30720 + 1024 * si + 1024] for si in range(2)]
    wostg = big1[:, 0:4096]
    wtstg = [big1[:, 4096 + 512 * si:4096 + 512 * si + 512] for si in range(2)]
    g_area = big1[:, 5120:7168]

    p.wait("sync", TT, TT.n)
    p.wait("sync", AVK, AVK.n)
    p.wait("sync", VV, VV.n)
    p.wait("sync", ST, ST.n)
    p.wait("scalar", ST, ST.n)
    p.wait("tensor", VV, VV.n)

    blk_tt = []
    hd_tt = []
    wosplit_vv = []
    wtsplit_vv = []
    ost_st = []
    smm_tt = []
    wt_hist = []
    silu_ss = []
    sg_vv = []
    xst_vv = []

    for tb in range(4):
        if tb >= 1:
            p.wait("sync", TT, hd_tt[-1])
        for comp, srcc in ((big0_16[:, 0:16384], xn_hi_src), (big0_16[:, 16384:32768], xn_lo_src)):
            for gq in range(2):
                p.dma("sync", comp.rearrange("p (c j) -> p c j", c=32, j=512)[:, 16 * gq:16 * gq + 16, :],
                      srcc[:, 16 * gq:16 * gq + 16, 512 * tb:512 * tb + 512], XNB)
        if tb >= 1:
            p.wait("scalar", TT, smm_tt[-1])
        p.dma("scalar", qtb_16[:, 0:4096].rearrange("p (c j) -> p c j", c=8, j=512),
              at_hi_src[:, :, 512 * tb:512 * tb + 512], ATB)
        p.dma("scalar", qtb_16[:, 4096:8192].rearrange("p (c j) -> p c j", c=8, j=512),
              at_lo_src[:, :, 512 * tb:512 * tb + 512], ATB)
        blk_ld = XNB.n
        blk_sld = ATB.n
        p.wait("tensor", XNB, blk_ld)
        p.wait("tensor", ATB, blk_sld)
        for o in range(8):
            osl = woslot[o % 2]
            wo_hi = osl[:, 0:4096]
            wo_lo = osl[:, 4096:8192]
            if wosplit_vv:
                p.wait("scalar", VV, wosplit_vv[-1])
            p.dma("scalar", wostg.rearrange("p (c j) -> p c j", c=8, j=512),
                  wo_src[:, :, 512 * o:512 * o + 512], WOS)
            wostg_ld = WOS.n
            if len(hd_tt) >= 2:
                p.wait("scalar", TT, hd_tt[-2])
            p.dma("scalar", bm_t[o % 2][:, :], bmat[0:R, 512 * o:512 * o + 512], BMS)
            bm_ld = BMS.n
            p.wait("vector", WOS, wostg_ld)
            if len(smm_tt) >= 2:
                p.wait("vector", TT, smm_tt[-2])
            p.op("vector", lambda e, wo_hi=wo_hi: e.tensor_copy(wo_hi, wostg))
            p.op("vector", lambda e, wo_hi=wo_hi, wo_lo=wo_lo: e.tensor_tensor(
                wo_lo, wostg, wo_hi, OP.subtract), VV)
            wosplit_vv.append(VV.n)
            for d in range(32):
                si = d % 2
                wtp = wtstg[si]
                if len(wtsplit_vv) >= 2:
                    p.wait("sync", VV, wtsplit_vv[-2])
                p.dma("sync", wtp, wT_src[:, d, 512 * o:512 * o + 512], WTS)
                p.wait("vector", WTS, WTS.n)
                if len(wt_hist) >= 2:
                    p.wait("vector", TT, wt_hist[-2])
                wts = wtslot[si]
                p.op("vector", lambda e, wts=wts, wtp=wtp: e.tensor_copy(wts[:, 0:512], wtp))
                p.op("vector", lambda e, wts=wts, wtp=wtp: e.tensor_tensor(
                    wts[:, 512:1024], wtp, wts[:, 0:512], OP.subtract), VV)
                wtsplit_vv.append(VV.n)
                p.wait("tensor", VV, wtsplit_vv[-1])
                if d == 0 and silu_ss:
                    p.wait("tensor", SS, silu_ss[-1])
                xh = big0_16[:, 0:16384]
                xl = big0_16[:, 16384:32768]
                for pi, (xa, woff2) in enumerate(((xh, 0), (xl, 0), (xh, 512))):
                    for ts in range(4):
                        p.op("tensor", lambda e, d=d, ts=ts, xa=xa, woff2=woff2, wts=wts, pi=pi: e.matmul(
                            pk[ts][:, 0:512], xa[:, 512 * d + 128 * ts:512 * d + 128 * ts + 128],
                            wts[:, woff2:woff2 + 512], start=(d == 0 and pi == 0), stop=False),
                            TT if (ts == 3 and pi == 2) else None)
                wt_hist.append(TT.n)
            for ts in range(4):
                p.op("tensor", lambda e, ts=ts, o=o: e.matmul(
                    pk[ts][:, 0:512], ones1[:, :], b_sb[0:1, 512 * o:512 * o + 512],
                    start=False, stop=False), None)
                if ts == 0:
                    p.wait("tensor", BMS, bm_ld)
                p.op("tensor", lambda e, o=o, ts=ts, tb=tb: e.matmul(
                    pk[ts][:, 0:512], zpm[:, 512 * tb + 128 * ts:512 * tb + 128 * ts + 128],
                    bm_t[o % 2][:, :], start=False, stop=True),
                    TT if ts == 3 else None)
            hdone_tt = TT.n
            hd_tt.append(hdone_tt)
            p.wait("scalar", TT, hdone_tt)
            if sg_vv:
                p.wait("scalar", VV, sg_vv[-1])
            for ts in range(4):
                p.op("scalar", lambda e, ts=ts: e.activation(
                    g_area[:, 512 * ts:512 * ts + 512], pk[ts][:, 0:512], ACT.Silu), SS)
            silu_ss.append(SS.n)
            p.wait("tensor", VV, wosplit_vv[-1])
            if sg_vv:
                p.wait("tensor", VV, sg_vv[-1])
            a_hi = qtb_16[:, 0:4096]
            a_lo = qtb_16[:, 4096:8192]
            for dv in range(8):
                for pi, (aa, woff2) in enumerate(((a_hi, 0), (a_lo, 0), (a_hi, 4096))):
                    for ts in range(4):
                        p.op("tensor", lambda e, dv=dv, ts=ts, aa=aa, woff2=woff2, osl=osl, pi=pi: e.matmul(
                            pk[4 + ts][:, 0:512],
                            aa[:, 512 * dv + 128 * ts:512 * dv + 128 * ts + 128],
                            osl[:, woff2 + 512 * dv:woff2 + 512 * dv + 512],
                            start=(dv == 0 and pi == 0), stop=(dv == 7 and pi == 2)),
                            TT if (dv == 7 and pi == 2 and ts == 3) else None)
            smm_tt.append(TT.n)
            for ts in range(4):
                xst = aring[:, 512 * (ts % 2):512 * (ts % 2) + 512]
                if len(xst_vv) >= 2:
                    p.wait("scalar", VV, xst_vv[-2])
                p.dma("scalar", xst, x_sl[512 * tb + 128 * ts:512 * tb + 128 * ts + 128,
                                          512 * o:512 * o + 512], XLS)
                xl = XLS.n
                if ts == 0:
                    p.wait("vector", TT, smm_tt[-1])
                    p.wait("vector", SS, silu_ss[-1])
                p.wait("vector", XLS, xl)
                ost = aring[:, 1024 + 512 * (ts % 2):1024 + 512 * (ts % 2) + 512]
                if len(ost_st) >= 2:
                    p.wait("vector", ST, ost_st[-2])
                p.op("vector", lambda e, ts=ts, ost=ost: e.tensor_tensor(
                    ost, pk[4 + ts][:, 0:512], g_area[:, 512 * ts:512 * ts + 512], OP.mult))
                p.op("vector", lambda e, ts=ts, ost=ost, xst=xst: e.tensor_tensor(ost, ost, xst, OP.add), VV)
                xst_vv.append(VV.n)
                p.wait("gpsimd", VV)
                p.dma("gpsimd", out_sl[512 * tb + 128 * ts:512 * tb + 128 * ts + 128,
                                       512 * o:512 * o + 512], ost, ST)
                ost_st.append(ST.n)
            sg_vv.append(VV.n)
        blk_tt.append(TT.n)

    return fin()


def make_inputs(inputs):
    x = np.ascontiguousarray(np.asarray(inputs["L0_attn_x"], dtype=np.float32))
    W = np.asarray(inputs["L0_ffn_W"], dtype=np.float32)
    common = {
        "wq": np.ascontiguousarray(np.asarray(inputs["L0_attn_wq"], dtype=np.float32)),
        "wk": np.ascontiguousarray(np.asarray(inputs["L0_attn_wk"], dtype=np.float32)),
        "wv": np.ascontiguousarray(np.asarray(inputs["L0_attn_wv"], dtype=np.float32)),
        "wo": np.ascontiguousarray(np.asarray(inputs["L0_attn_wo"], dtype=np.float32)),
        "wT": np.ascontiguousarray(W.T),
        "w8": np.ascontiguousarray(W[0:E, :]),
        "bvec": np.ascontiguousarray(np.asarray(inputs["L0_ffn_b"], dtype=np.float32)).reshape(1, D),
        "amat": np.ascontiguousarray(np.asarray(inputs["L0_ffn_A"], dtype=np.float32)),
        "bmat": np.ascontiguousarray(np.asarray(inputs["L0_ffn_Bmat"], dtype=np.float32)),
        "limes": np.ascontiguousarray(np.asarray(inputs["L0_ffn_LiMEs"], dtype=np.float32)),
        "limesT": np.ascontiguousarray(np.asarray(inputs["L0_ffn_LiMEs"], dtype=np.float32).T),
        "identH": np.eye(128, dtype=np.float32),
    }
    in_maps = []
    for c in range(8):
        b, t0 = c // 4, (c % 4) * T
        m = dict(common)
        m["x_sl"] = np.ascontiguousarray(x[b, t0:t0 + T, :])
        in_maps.append(m)
    return in_maps


def kernel(**inputs):
    in_maps = make_inputs(inputs)
    nc, _ = build(dbg=False, stage=4)
    res = run_bass_kernel_spmd(nc, in_maps, core_ids=list(range(8)))
    out = np.empty((2, S, D), dtype=np.float32)
    for c in range(8):
        b, t0 = c // 4, (c % 4) * T
        out[b, t0:t0 + T, :] = np.asarray(res.results[c]["out_sl"])
    return out
